# version 07
# baseline (speedup 1.0000x reference)
import numpy as np
import concourse.bass as bass
import concourse.mybir as mybir
from concourse.bass_utils import run_bass_kernel_spmd
from concourse.alu_op_type import AluOpType as ALU

F32 = mybir.dt.float32
BF16 = mybir.dt.bfloat16
AF = mybir.ActivationFunctionType

DM, DI, DS, NH, HD = 1024, 2048, 64, 16, 128
NFC = 8
NCC = 17
TPC = 2048
GQ = 512
NG = 4
Q = 128
NCH = 4
NZC, NDC, NKC, NHC = 16, 8, 16, 16
SW1 = 2192

_DEBUG = 0
_DBG_G = 0
ARENA_COLS = 53000



def _prep_shared(inp):
    f = lambda x: np.ascontiguousarray(np.asarray(x, np.float32))
    d = {}
    WipT = f(inp["L0_attn_in_proj_w"]).T
    d["wsw"] = np.ascontiguousarray(
        np.concatenate([WipT[:, 2048:4096], WipT[:, 4096:4160],
                        WipT[:, 4160:4224], WipT[:, 4224:4240]], axis=1))
    d["wz"] = np.ascontiguousarray(
        WipT[:, 0:2048].reshape(8, 128, 16, 128).transpose(2, 1, 0, 3))
    d["wo"] = np.ascontiguousarray(
        f(inp["L0_attn_out_w"]).T.reshape(16, 128, 8, 128).transpose(2, 1, 0, 3))
    d["convw"] = np.ascontiguousarray(
        f(inp["L0_attn_conv_w"]).reshape(17, 128, 4).transpose(1, 0, 2)).reshape(128, 68)
    d["convb"] = np.ascontiguousarray(f(inp["L0_attn_conv_b"]).reshape(17, 128).T)
    d["dtb"] = f(inp["L0_attn_dt_bias"]).reshape(16, 1)
    d["alog16"] = f(inp["L0_attn_A_log"]).reshape(16, 1)
    d["alogr"] = f(inp["L0_attn_A_log"]).reshape(1, 16)
    d["drow"] = f(inp["L0_attn_D"]).reshape(1, 16)
    d["rmsw"] = np.ascontiguousarray(f(inp["L0_attn_rms_w"]).reshape(16, 128).T)
    d["rw"] = np.ascontiguousarray(
        f(inp["L0_ffn_router_w"]).reshape(8, 128, 8).transpose(1, 0, 2)).reshape(128, 64)
    d["rb"] = f(inp["L0_ffn_router_b"]).reshape(8, 1)
    w1c = np.concatenate([f(inp["L0_ffn_w1"]), f(inp["L0_ffn_sw1"])], axis=0)
    d["w1"] = np.ascontiguousarray(
        w1c.reshape(10, 8, 128, 16, 128).transpose(0, 3, 2, 1, 4))
    w2c = np.concatenate([f(inp["L0_ffn_w2"]), f(inp["L0_ffn_sw2"])], axis=0)
    d["w2"] = np.ascontiguousarray(
        w2c.reshape(10, 16, 128, 8, 128).transpose(0, 3, 2, 1, 4))
    b1c = np.concatenate([f(inp["L0_ffn_b1"]), f(inp["L0_ffn_sb1"])], axis=0)
    d["b1"] = np.ascontiguousarray(
        b1c.reshape(10, 16, 128).transpose(2, 0, 1)).reshape(128, 160)
    b2c = np.concatenate([f(inp["L0_ffn_b2"]), f(inp["L0_ffn_sb2"])], axis=0)
    d["b2"] = np.ascontiguousarray(
        np.concatenate([b2c, np.zeros((6, DM), np.float32)], axis=0))
    d["identf"] = np.eye(128, dtype=np.float32)
    triu = np.triu(np.ones((128, 128), np.float32))
    d["triu"] = triu
    d["maskneg"] = np.where(triu > 0, 0.0, -1e30).astype(np.float32)
    d["onesf"] = np.ones((128, 128), np.float32)
    d["onescol"] = np.ones((128, 1), np.float32)
    d["cvals"] = np.tile(np.array([[1e-5, 1.0, -0.5, -1.0, 1.0 / 1024,
                                    1.0 / 2048, 0.5, 0.0]], np.float32),
                         (128, 1))
    d["negonesf"] = -np.ones((128, 128), np.float32)
    return d


def _prep_core(u, core):
    b, half = core // 2, core % 2
    z3 = np.zeros((3, DM), np.float32)
    if half == 0:
        uo = np.concatenate([z3, u[b, 0:TPC]], axis=0)
        up, mask = uo, 0.0
    else:
        uo = u[b, TPC - 3:2 * TPC]
        up = np.concatenate([z3, u[b, 0:TPC]], axis=0)
        mask = 1.0
    return {"u_own": np.ascontiguousarray(uo.T),
            "u_prev": np.ascontiguousarray(up.T),
            "smask": np.full((64, 1), mask, np.float32)}



class Sched:
    ENG = ("pe", "act", "dve", "pool", "sp")

    def __init__(self):
        self.items = {e: [] for e in self.ENG}
        self.cnt = dict.fromkeys(list(self.ENG) + ["dsp", "dpool"], 0)
        self.hwm = {e: {} for e in self.ENG}

    def op(self, eng, fn, deps=(), pub=False, dma=False):
        for tok in deps:
            if tok is None:
                continue
            s, v = tok
            if s == eng:
                continue
            if v > self.hwm[eng].get(s, 0):
                self.items[eng].append(("wait", s, v))
                self.hwm[eng][s] = v
        if dma:
            sem, amt = ("dsp" if eng == "sp" else "dpool"), 16
        elif pub:
            sem, amt = eng, 1
        else:
            sem = None
        if sem is not None:
            self.cnt[sem] += amt
            tok = (sem, self.cnt[sem])
        else:
            tok = (eng, self.cnt[eng])
        self.items[eng].append(("op", fn, sem, amt if sem else 0))
        return tok

    def wait(self, eng, tok):
        if tok is None:
            return
        s, v = tok
        if s == eng:
            return
        if v > self.hwm[eng].get(s, 0):
            self.items[eng].append(("wait", s, v))
            self.hwm[eng][s] = v

    def emit(self, nc, sems):
        S = self

        def mk(name):
            def body(h):
                for it in S.items[name]:
                    if it[0] == "wait":
                        h.wait_ge(sems[it[1]], it[2])
                    else:
                        inst = it[1](h)
                        if it[2] is not None:
                            inst.then_inc(sems[it[2]], it[3])
            return body
        with nc.Block() as block:
            block.tensor(mk("pe"))
            block.scalar(mk("act"))
            block.vector(mk("dve"))
            block.gpsimd(mk("pool"))
            block.sync(mk("sp"))


class Buf:
    __slots__ = ("wtok", "rtoks", "fresh", "widx", "nw")

    def __init__(self):
        self.wtok = None
        self.rtoks = []
        self.fresh = True
        self.widx = None
        self.nw = 9999


class Arena:
    def __init__(self, ap):
        self.ap = ap
        self.off = 0
        self.marks = {}

    def f32(self, cols, parts=128):
        c0 = self.off
        self.off += cols
        assert self.off <= ARENA_COLS, ("arena overflow", self.off)
        return self.ap[0:parts, c0:c0 + cols]

    def bf16(self, cols, parts=128):
        cf = (cols + 1) // 2
        c0 = self.off
        self.off += cf
        assert self.off <= ARENA_COLS, ("arena overflow", self.off)
        return self.ap[0:parts, c0:c0 + cf].bitcast(BF16)[0:parts, 0:cols]

    def mark(self, n):
        self.marks[n] = self.off

    def reset(self, n):
        self.off = self.marks[n]


class K:
    GUARD_DIST = 6
    GUARD_W = 64

    def __init__(self):
        self.S = Sched()
        self.seq = dict.fromkeys(Sched.ENG, 0)
        self.ndrain = 0

    def do(self, eng, fn, reads=(), writes=(), pub=True, dma=False, ow=9999):
        if eng in ("dve", "act", "pool") and not dma:
            for b in list(reads) + list(writes):
                if (b.widx is not None and b.widx[0] == eng
                        and b.nw < self.GUARD_W
                        and self.seq[eng] - b.widx[1] <= self.GUARD_DIST):
                    self.S.op(eng, lambda e: e.drain(), ())
                    self.seq[eng] += 1
                    self.ndrain += 1
                    break
        deps = []
        for b in reads:
            deps.append(b.wtok)
        for b in writes:
            deps.append(b.wtok)
            deps.extend(b.rtoks)
        tok = self.S.op(eng, fn, deps, pub=pub, dma=dma)
        self.seq[eng] += 1
        for b in writes:
            b.wtok = tok
            b.rtoks = []
            b.widx = (eng, self.seq[eng])
            b.nw = ow
        for b in reads:
            b.rtoks.append(tok)
        return tok

    def mm(self, bank, out, lhsT, rhs, reads, last, stop=None, pub=None):
        st = bank.fresh
        bank.fresh = False
        sp = last if stop is None else stop
        pb = last if pub is None else pub
        return self.do("pe",
                       lambda e, o=out, l=lhsT, r=rhs, a=st, b=sp:
                       e.matmul(o, l, r, start=a, stop=b),
                       reads=reads, writes=[bank], pub=pb)

    def transp(self, bank, out, in_, ident, reads, pub=True):
        bank.fresh = False
        return self.do("pe",
                       lambda e, o=out, i=in_, d=ident: e.transpose(o, i, d),
                       reads=reads, writes=[bank], pub=pub)

    def act(self, fn, out, in_, reads, writes, bias=0.0, scale=1.0, pub=True):
        return self.do("act",
                       lambda e, o=out, i=in_, f=fn, b=bias, s=scale:
                       e.activation(o, i, f, bias=b, scale=s),
                       reads=reads, writes=writes, pub=pub,
                       ow=int(np.prod(out.shape[1:])))

    def copy(self, out, in_, reads, writes, eng="dve", pub=True):
        if eng == "act":
            fn = lambda e, o=out, i=in_: e.copy(o, i)
        else:
            fn = lambda e, o=out, i=in_: e.tensor_copy(o, i)
        return self.do(eng, fn, reads=reads, writes=writes, pub=pub,
                       ow=int(np.prod(out.shape[1:])))

    def tt(self, out, i0, i1, op, reads, writes, eng="dve", pub=True):
        return self.do(eng, lambda e, o=out, a=i0, b=i1, p=op:
                       e.tensor_tensor(o, a, b, p),
                       reads=reads, writes=writes, pub=pub,
                       ow=int(np.prod(out.shape[1:])))

    def ts1(self, out, i0, s1, op0, reads, writes, eng="dve", pub=True):
        return self.do(eng, lambda e, o=out, a=i0, x=s1, p0=op0:
                       e.tensor_scalar(o, a, x, None, p0),
                       reads=reads, writes=writes, pub=pub,
                       ow=int(np.prod(out.shape[1:])))

    def ts2(self, out, i0, s1, s2, op0, op1, reads, writes, eng="dve", pub=True):
        return self.do(eng, lambda e, o=out, a=i0, x=s1, y=s2, p0=op0, p1=op1:
                       e.tensor_scalar(o, a, x, y, p0, p1),
                       reads=reads, writes=writes, pub=pub,
                       ow=int(np.prod(out.shape[1:])))

    def stt(self, out, i0, sc, i1, op0, op1, reads, writes, eng="dve", pub=True):
        return self.do(eng, lambda e, o=out, a=i0, x=sc, b=i1, p0=op0, p1=op1:
                       e.scalar_tensor_tensor(o, a, x, b, p0, p1),
                       reads=reads, writes=writes, pub=pub,
                       ow=int(np.prod(out.shape[1:])))

    def red(self, out, in_, op, reads, writes, eng="dve", pub=True):
        return self.do(eng, lambda e, o=out, i=in_, p=op:
                       e.tensor_reduce(o, i, mybir.AxisListType.X, p),
                       reads=reads, writes=writes, pub=pub,
                       ow=int(np.prod(out.shape[1:])))

    def dma(self, out, in_, reads=(), writes=(), eng="sp", accum=False):
        if accum:
            fn = lambda e, o=out, i=in_: e.dma_start(out=o, in_=i,
                                                     accum_op=ALU.add)
        else:
            fn = lambda e, o=out, i=in_: e.dma_start(out=o, in_=i)
        return self.do(eng, fn, reads=reads, writes=writes, dma=True)

    def memset(self, ap_, val, reads=(), writes=(), eng="pool", pub=True):
        return self.do(eng, lambda e, a=ap_, v=val: e.memset(a, v),
                       reads=reads, writes=writes, pub=pub)



def build_program():
    nc = bass.Bass("TRN2", target_bir_lowering=False)
    k = K()

    def din(name, shape):
        return nc.dram_tensor(name, list(shape), F32, kind="ExternalInput").ap()

    D = {n: din(n, s) for n, s in [
        ("u_own", (DM, 2051)), ("u_prev", (DM, 2051)), ("smask", (64, 1)),
        ("wsw", (DM, SW1)), ("wz", (16, 128, 8, 128)), ("wo", (8, 128, 16, 128)),
        ("convw", (128, 68)), ("convb", (128, 17)), ("dtb", (16, 1)),
        ("alog16", (16, 1)), ("alogr", (1, 16)), ("drow", (1, 16)),
        ("rmsw", (128, 16)), ("rw", (128, 64)), ("rb", (8, 1)),
        ("w1", (10, 16, 128, 8, 128)), ("w2", (10, 8, 128, 16, 128)),
        ("b1", (128, 160)), ("b2", (16, DM)), ("identf", (128, 128)),
        ("triu", (128, 128)), ("maskneg", (128, 128)), ("onesf", (128, 128)),
        ("negonesf", (128, 128)), ("cvals", (128, 8)),
        ("onescol", (128, 1))]}
    scratch = nc.dram_tensor("scr", [DM, TPC], F32, kind="Internal").ap()
    out_d = nc.dram_tensor("out", [DM, TPC], F32, kind="ExternalOutput").ap()
    dbg = {}
    if _DEBUG:
        def dout(name, shape, dt):
            dbg[name] = nc.dram_tensor(name, list(shape), dt,
                                       kind="ExternalOutput").ap()
        dout("dbg_xn", (128, NFC * 2051), BF16)
        dout("dbg_gates", (8, TPC), F32)
        dout("dbg_xbc", (128, NCC * GQ), BF16)
        dout("dbg_dt", (16, GQ), F32)
        dout("dbg_y", (128, NZC * GQ), BF16)
        dout("dbg_h", (64, NH * HD), F32)
        dout("dbg_hp", (64, NH * HD), F32)
        dout("dbg_scr", (DM, TPC), F32)
        dout("dbg_ws", (128, 16), F32)
        dout("dbg_dec", (64, 16), F32)
        dout("dbg_cta", (128, 16), F32)
        dout("dbg_cq", (1, 16), F32)
        dout("dbg_h0", (64, NH * HD), F32)
        dout("dbg_dtvp", (16, GQ), F32)
        dout("dbg_xbcp", (128, NCC * GQ), BF16)
        dout("dbg_bt", (128, DS), BF16)
        dout("dbg_xt", (128, NH * Q), BF16)
        dout("dbg_ps5", (128, 16), F32)
        dout("dbg_rows", (1, 1545), F32)
        dout("dbg_xn2", (128, NFC * 2051), BF16)

    ctx = []

    def enter(cm):
        v = cm.__enter__()
        ctx.append(cm)
        return v

    arena = enter(nc.sbuf_tensor("arena", [128, ARENA_COLS], F32))[:, :]
    A = Arena(arena)
    ps = [enter(nc.psum_tensor("ps%d" % i, [128, 512], F32))[:, :]
          for i in range(8)]
    sems = {n: enter(nc.semaphore("sem_" + n))
            for n in ("pe", "act", "dve", "pool", "dsp", "dpool")}

    PSB = [Buf() for _ in range(8)]

    def release(i):
        PSB[i].fresh = True

    identf = A.f32(128)
    identb = A.bf16(128)
    triu_b = A.bf16(128)
    maskneg = A.f32(128)
    onesb = A.bf16(128)
    negonesb = A.bf16(128)
    onesf_c = A.f32(1)
    onesb_c = A.bf16(2)[:, 0:1]
    onesf_r = A.f32(128, parts=1)
    onesb_r = A.bf16(128, parts=1)
    cvals_s = A.f32(8)
    xn = A.bf16(NFC * 2051)
    gates = A.f32(TPC, parts=8)
    A.mark("perm")

    convw = A.f32(68)
    convb = A.f32(17)
    dtb = A.f32(1, parts=16)
    alog16 = A.f32(1, parts=16)
    alogr = A.f32(16, parts=1)
    abc128 = A.f32(16)
    dbc128 = A.f32(16)
    drow_s = A.f32(16, parts=1)
    rmsw = A.f32(16)
    rw = A.f32(64)
    rb = A.f32(1, parts=8)
    csrw = A.f32(1, parts=8)
    smask_s = A.f32(1, parts=64)
    H = A.f32(NH * HD, parts=64)
    Hbf = A.bf16(NH * HD, parts=64)
    sumsq = A.f32(GQ, parts=1)
    erow = A.bf16(NH * Q, parts=1)
    cqrow = A.f32(16, parts=1)
    dtsum = A.f32(1, parts=16)
    ug = A.f32(NFC * 515)
    t1n = A.f32(515)
    sqln = A.bf16(515)
    rows = A.f32(1545, parts=1)
    wsw = A.bf16(NFC * SW1)
    preconv = [A.f32(515)]
    convacc = A.f32(512)
    xbc = A.bf16(NCC * GQ)
    dtg = A.f32(515, parts=16)
    dtv = A.f32(GQ, parts=16)
    dtT = [A.f32(16), A.f32(16)]
    dta = [A.bf16(NH * Q), A.bf16(NH * Q)]
    ctA = [A.f32(16), A.f32(16)]
    wS = [A.f32(16), A.f32(16)]
    decay = [A.f32(16, parts=64), A.f32(16, parts=64)]
    bT = [A.bf16(DS), A.bf16(DS)]
    bwt = [A.bf16(DS), A.bf16(DS)]
    xT = [A.bf16(NH * Q)]
    lmask = [A.f32(Q), A.f32(Q)]
    expl = [A.f32(Q), A.f32(Q)]
    stl = [A.bf16(Q), A.bf16(Q)]
    cw = [A.bf16(Q, parts=64), A.bf16(Q, parts=64)]
    yg = A.bf16(NZC * GQ)
    sqs = A.bf16(4 * Q)
    ccpy = A.bf16(Q, parts=64)
    zsig = [A.bf16(GQ), A.bf16(GQ)]
    wzp = [A.bf16(NFC * 128), A.bf16(NFC * 128)]
    rstd_bc = A.f32(GQ)
    ostage = [A.f32(GQ), A.f32(GQ)]
    A.mark("ap_shared")
    xnp = A.bf16(NFC * 515)
    p_high = A.off
    A.reset("ap_shared")
    wop = [A.bf16(NKC * 128), A.bf16(NKC * 128)]
    a_high = A.off
    A.off = max(a_high, p_high)
    A.mark("ap_end")

    A.reset("perm")
    acc = A.f32(NDC * TPC)
    hsb = A.bf16(NHC * TPC)
    w1p = [A.bf16(NFC * 128) for _ in range(3)]
    w2p = [A.bf16(NKC * 128) for _ in range(2)]
    hraw = [A.bf16(GQ), A.bf16(GQ)]
    combtok = A.f32(160)
    comb10 = A.bf16(TPC, parts=16)
    combrow = A.bf16(TPC, parts=1)
    combbc = [A.bf16(TPC), A.bf16(TPC)]
    b1c = A.f32(160)
    b2c = A.bf16(DM, parts=16)
    gtt = A.f32(8)
    gsm = A.f32(8)
    is1t = A.f32(8)
    g2t = A.f32(8)
    is2t = A.f32(8)
    tmp8 = A.f32(8)
    f_high = A.off
    assert max(A.marks["ap_end"], f_high) <= ARENA_COLS

    xn_v = xn.rearrange("p (a b) -> p a b", a=NFC)
    xnp_v = xnp.rearrange("p (a b) -> p a b", a=NFC)
    ug_v = ug.rearrange("p (a b) -> p a b", a=NFC)
    wsw_v = wsw.rearrange("p (a b) -> p a b", a=NFC)
    rw_v = rw.rearrange("p (a b) -> p a b", a=NFC)
    xbc_v = xbc.rearrange("p (a b) -> p a b", a=NCC)
    yg_v = yg.rearrange("p (a b) -> p a b", a=NZC)
    acc_v = acc.rearrange("p (a b) -> p a b", a=NDC)
    hsb_v = hsb.rearrange("p (a b) -> p a b", a=NHC)

    NB = lambda: Buf()
    bconst = NB()
    bug, bxn, bxnp, bgates, bgtmp = NB(), NB(), NB(), NB(), NB()
    brows, bsqln, bt1n, brst = NB(), NB(), NB(), NB()
    bwsw = NB()
    bpre = [NB()]
    bconvacc = NB()
    bxbc, bdtg, bdtv = NB(), NB(), NB()
    bdtT = [NB(), NB()]
    bdta = [NB(), NB()]
    bctA = [NB(), NB()]
    bwS = [NB(), NB()]
    bdecay = [NB(), NB()]
    bbT = [NB(), NB()]
    bbw = [NB(), NB()]
    bxT = [NB()]
    blm = [NB(), NB()]
    bexpl = [NB(), NB()]
    bstl = [NB(), NB()]
    bcw = [NB(), NB()]
    byg, bsqs = NB(), NB()
    bccpy = NB()
    bzsig = [NB(), NB()]
    bwzp = [NB(), NB()]
    bwop = [NB(), NB()]
    brstdbc = NB()
    bost = [NB(), NB()]
    bH, bHbf = NB(), NB()
    bsumsq, berow, bcq, bdtsum = NB(), NB(), NB(), NB()
    bscr, bout = NB(), NB()
    bacc, bhsb = NB(), NB()
    bw1p = [NB(), NB(), NB()]
    bw2p = [NB(), NB()]
    bhraw = [NB(), NB()]
    bcombtok, bcomb10, bcombrow = NB(), NB(), NB()
    bcombbc = [NB(), NB()]
    bgtt, bgsm = NB(), NB()
    bis1, bg2, bis2, btmp8 = NB(), NB(), NB(), NB()

    for dst, src in [(identf, "identf"), (maskneg, "maskneg"), (convw, "convw"),
                     (convb, "convb"), (dtb, "dtb"), (alog16, "alog16"),
                     (alogr, "alogr"), (rmsw, "rmsw"), (rw, "rw"), (rb, "rb"),
                     (drow_s, "drow"), (smask_s, "smask")]:
        k.dma(dst, D[src], writes=[bconst])
    for dst, src in [(identb, "identf"), (triu_b, "triu"), (onesb, "onesf"),
                     (negonesb, "negonesf")]:
        k.dma(dst, D[src], writes=[bconst], eng="pool")
    k.dma(onesf_c, D["onescol"], writes=[bconst], eng="pool")
    k.dma(onesb_c, D["onescol"], writes=[bconst], eng="pool")
    k.dma(onesf_r, D["onesf"][0:1, :], writes=[bconst], eng="pool")
    k.dma(onesb_r, D["onesf"][0:1, :], writes=[bconst], eng="pool")
    for fc in range(NFC):
        k.dma(wsw_v[:, fc, :], D["wsw"][fc * 128:(fc + 1) * 128, :],
              writes=[bwsw], eng="pool")
    k.memset(H[0:64, :], 0.0, writes=[bH])
    tcv = k.dma(cvals_s, D["cvals"], writes=[bconst])
    for i, v in enumerate([1e-5, 1.0, -0.5, -1.0, 1.0 / 1024, 1.0 / 2048,
                           0.5, 0.0]):
        nc.const_aps.register(F32, v, cvals_s[:, i:i + 1])
    for e in ("pe", "act", "dve", "pool"):
        k.S.wait(e, tcv)

    k.act(AF.Exp, alog16, alog16, reads=[bconst], writes=[bconst])
    k.ts1(alog16, alog16, -1.0, ALU.mult, reads=[], writes=[bconst])
    k.act(AF.Exp, alogr, alogr, reads=[bconst], writes=[bconst])
    k.ts1(alogr, alogr, -1.0, ALU.mult, reads=[], writes=[bconst])
    k.mm(PSB[6], ps[6][:, 0:16], onesf_r, alogr, [bconst], last=True)
    k.copy(abc128, ps[6][:, 0:16], reads=[PSB[6]], writes=[bconst])
    release(6)
    for fc in range(NFC):
        k.mm(PSB[6], ps[6][0:8, 16:17], rw_v[:, fc, :], onesf_c, [bconst],
             last=(fc == 7))
    k.copy(csrw[0:8, 0:1], ps[6][0:8, 16:17], reads=[PSB[6]], writes=[bconst])
    release(6)
    k.mm(PSB[7], ps[7][:, 0:16], onesf_r, drow_s, [bconst], last=True)
    k.copy(dbc128, ps[7][:, 0:16], reads=[PSB[7]], writes=[bconst])
    release(7)

    def front_stats(u_dram, g, xdst, bxd, xw0, do_gates):
        w0 = g * GQ
        uin = u_dram.rearrange("(a p) n -> p a n", p=128)[:, :, w0:w0 + 515]
        k.dma(ug_v, uin, writes=[bug])
        for fc in range(NFC):
            k.act(AF.Square, sqln, ug_v[:, fc, :], reads=[bug], writes=[bsqln])
            k.mm(PSB[4], ps[4][0:1, 0:512], onesf_c, ug_v[:, fc, 0:512],
                 [bug, bconst], last=(fc == 7), pub=True)
            k.mm(PSB[5], ps[5][0:1, 0:3], onesf_c, ug_v[:, fc, 512:515],
                 [bug, bconst], last=False, stop=(fc == 7), pub=True)
            k.mm(PSB[6], ps[6][0:1, 0:512], onesb_c, sqln[:, 0:512],
                 [bsqln, bconst], last=(fc == 7), pub=True)
            k.mm(PSB[5], ps[5][0:1, 4:7], onesb_c, sqln[:, 512:515],
                 [bsqln, bconst], last=(fc == 7), pub=True)
        mean_r = rows[0:1, 0:515]
        msq_r = rows[0:1, 515:1030]
        rstd_r = rows[0:1, 1030:1545]
        k.act(AF.Copy, mean_r[0:1, 0:512], ps[4][0:1, 0:512], reads=[PSB[4]],
              writes=[brows], scale=1.0 / DM)
        k.act(AF.Copy, mean_r[0:1, 512:515], ps[5][0:1, 0:3], reads=[PSB[5]],
              writes=[brows], scale=1.0 / DM)
        k.act(AF.Copy, msq_r[0:1, 0:512], ps[6][0:1, 0:512], reads=[PSB[6]],
              writes=[brows], scale=1.0 / DM)
        k.act(AF.Copy, msq_r[0:1, 512:515], ps[5][0:1, 4:7], reads=[PSB[5]],
              writes=[brows], scale=1.0 / DM)
        release(4), release(5), release(6)
        k.tt(rstd_r, mean_r, mean_r, ALU.mult, reads=[brows], writes=[brows])
        k.tt(rstd_r, msq_r, rstd_r, ALU.subtract, reads=[], writes=[brows])
        k.act(AF.Ln, rstd_r, rstd_r, reads=[brows], writes=[brows], bias=1e-5)
        k.act(AF.Exp, rstd_r, rstd_r, reads=[], writes=[brows], scale=-0.5)
        k.tt(mean_r, mean_r, rstd_r, ALU.mult, reads=[brows], writes=[brows])
        mrs_r = mean_r
        k.mm(PSB[4], ps[4][:, 0:512], onesf_r, rstd_r[0:1, 0:512], [brows],
             last=True)
        k.mm(PSB[5], ps[5][:, 8:11], onesf_r, rstd_r[0:1, 512:515], [brows],
             last=False, stop=True)
        k.mm(PSB[6], ps[6][:, 0:512], onesf_r, mrs_r[0:1, 0:512], [brows],
             last=True)
        k.mm(PSB[5], ps[5][:, 12:15], onesf_r, mrs_r[0:1, 512:515], [brows],
             last=True)
        if _DEBUG and do_gates and g == 3:
            k.copy(ostage[1][:, 0:16], ps[5][:, 0:16], reads=[PSB[5]],
                   writes=[bost[1]])
            k.dma(dbg["dbg_ps5"], ostage[1][:, 0:16], reads=[bost[1]])
            k.dma(dbg["dbg_rows"], rows[0:1, :], reads=[brows])
        for fc in range(NFC):
            k.tt(t1n[:, 0:512], ug_v[:, fc, 0:512], ps[4][:, 0:512], ALU.mult,
                 reads=[bug, PSB[4]], writes=[bt1n])
            k.tt(xdst[:, fc, xw0:xw0 + 512], t1n[:, 0:512], ps[6][:, 0:512],
                 ALU.subtract, reads=[bt1n, PSB[6]], writes=[bxd])
            k.tt(t1n[:, 512:515], ug_v[:, fc, 512:515], ps[5][:, 8:11],
                 ALU.mult, reads=[bug, PSB[5]], writes=[bt1n])
            k.tt(xdst[:, fc, xw0 + 512:xw0 + 515], t1n[:, 512:515],
                 ps[5][:, 12:15], ALU.subtract, reads=[bt1n, PSB[5]],
                 writes=[bxd])
        if _DEBUG and do_gates and g == 3:
            k.dma(dbg["dbg_xn2"], xn, reads=[bxn])
        if do_gates:
            for fc in range(NFC):
                k.mm(PSB[7], ps[7][0:8, 0:512], rw_v[:, fc, :],
                     ug_v[:, fc, 3:515], [bug, bconst], last=(fc == 7))
            g0 = gates[0:8, w0:w0 + 512]
            k.copy(g0, ps[7][0:8, 0:512], reads=[PSB[7]], writes=[bgates])
            release(7)
            k.mm(PSB[7], ps[7][0:8, 0:512], onesf_r[0:1, 0:8],
                 rstd_r[0:1, 3:515], [brows], last=True)
            k.tt(g0, g0, ps[7][0:8, 0:512], ALU.mult, reads=[PSB[7]],
                 writes=[bgates])
            release(7)
            k.mm(PSB[7], ps[7][0:8, 0:512], onesf_r[0:1, 0:8],
                 mrs_r[0:1, 3:515], [brows], last=True)
            gtmp = ostage[0][0:8, :]
            k.ts1(gtmp, ps[7][0:8, 0:512], csrw[0:8, 0:1], ALU.mult,
                  reads=[PSB[7], bconst], writes=[bost[0]])
            release(7)
            k.tt(g0, g0, gtmp, ALU.subtract, reads=[bost[0]],
                 writes=[bgates])
            k.ts1(g0, g0, rb[0:8, 0:1], ALU.add, reads=[bconst],
                  writes=[bgates])
        release(4), release(5), release(6)

    def front_mm(xdst, bxd, xw0):
        for cc in range(NCC + 1):
            M = 128 if cc < NCC else 16
            pa, pb = ps[cc % 2], ps[2 + cc % 2]
            ba, bb = PSB[cc % 2], PSB[2 + cc % 2]
            for fc in range(NFC):
                lhs = wsw_v[:, fc, cc * 128:cc * 128 + M]
                k.mm(ba, pa[0:M, 0:512], lhs, xdst[:, fc, xw0:xw0 + 512],
                     [bxd, bwsw], last=(fc == 7))
                k.mm(bb, pb[0:M, 0:3], lhs, xdst[:, fc, xw0 + 512:xw0 + 515],
                     [bxd, bwsw], last=(fc == 7))
            if cc < NCC:
                pc, bpc = preconv[0], bpre[0]
                k.act(AF.Copy, pc[:, 0:512], pa[0:128, 0:512], reads=[ba],
                      writes=[bpc])
                k.act(AF.Copy, pc[:, 512:515], pb[0:128, 0:3], reads=[bb],
                      writes=[bpc])
                release(cc % 2), release(2 + cc % 2)
                ceng = "dve"
                k.ts1(convacc, pc[:, 0:512], convw[:, 4 * cc:4 * cc + 1],
                      ALU.mult, reads=[bpc, bconst], writes=[bconvacc],
                      eng=ceng)
                for t in range(1, 4):
                    k.stt(convacc, pc[:, t:512 + t],
                          convw[:, 4 * cc + t:4 * cc + t + 1], convacc,
                          ALU.mult, ALU.add, reads=[bpc, bconst],
                          writes=[bconvacc], eng=ceng)
                k.act(AF.Silu, xbc_v[:, cc, :], convacc, reads=[bconvacc],
                      writes=[bxbc], bias=convb[:, cc:cc + 1])
            else:
                k.act(AF.Copy, dtg[0:16, 0:512], pa[0:16, 0:512], reads=[ba],
                      writes=[bdtg])
                k.act(AF.Copy, dtg[0:16, 512:515], pb[0:16, 0:3], reads=[bb],
                      writes=[bdtg])
                release(cc % 2), release(2 + cc % 2)
                k.ts2(dtv[0:16, :], dtg[0:16, 3:515], dtb[0:16, 0:1], 0.0,
                      ALU.add, ALU.max, reads=[bdtg, bconst], writes=[bdtv])
                k.act(AF.Abs, dtg[0:16, 3:515], dtg[0:16, 3:515], reads=[bdtg],
                      writes=[bdtg], bias=dtb[0:16, 0:1])
                k.act(AF.Exp, dtg[0:16, 3:515], dtg[0:16, 3:515], reads=[],
                      writes=[bdtg], scale=-1.0)
                k.act(AF.Ln, dtg[0:16, 3:515], dtg[0:16, 3:515], reads=[],
                      writes=[bdtg], bias=1.0)
                k.tt(dtv[0:16, :], dtv[0:16, :], dtg[0:16, 3:515], ALU.add,
                     reads=[bdtg], writes=[bdtv])

    def chunk(gc, c, do_y):
        par = gc % 2
        c0 = c * Q
        dtc = dtv[0:16, c0:c0 + Q]
        k.transp(PSB[4], ps[4][:, 0:16], dtc, identf[0:16, 0:16],
                 [bdtv, bconst], pub=True)
        k.copy(dtT[par], ps[4][:, 0:16], reads=[PSB[4]], writes=[bdtT[par]], eng="act")
        for h in range(NH):
            k.ts2(dta[par][:, h * Q:(h + 1) * Q], triu_b, dtT[par][:, h:h + 1],
                  abc128[:, h:h + 1], ALU.mult, ALU.mult,
                  reads=[bconst, bdtT[par]], writes=[bdta[par]],
                  pub=(h % 4 == 3))
        for h in range(NH):
            k.mm(PSB[4], ps[4][0:1, 64 + h:65 + h], onesb_c,
                 dta[par][:, h * Q + Q - 1:h * Q + Q], [bdta[par], bconst],
                 last=(h == 15), pub=True)
        k.copy(cqrow, ps[4][0:1, 64:80], reads=[PSB[4]], writes=[bcq], eng="act")
        for h in range(NH):
            k.mm(PSB[4], ps[4][:, 16 + h:17 + h],
                 dta[par][:, h * Q:(h + 1) * Q], onesb_c, [bdta[par], bconst],
                 last=(h == 15))
        k.copy(ctA[par], ps[4][:, 16:32], reads=[PSB[4]], writes=[bctA[par]], eng="act")
        k.mm(PSB[4], ps[4][:, 32:48], onesf_r, cqrow, [bcq], last=True)
        k.stt(wS[par], ctA[par], -1.0, ps[4][:, 32:48], ALU.mult, ALU.add,
              reads=[bctA[par], PSB[4]], writes=[bwS[par]])
        k.act(AF.Exp, wS[par], wS[par], reads=[bwS[par]], writes=[bwS[par]])
        k.mm(PSB[4], ps[4][0:64, 48:64], onesf_r[0:1, 0:64], cqrow, [bcq],
             last=True)
        k.act(AF.Exp, decay[par][0:64, :], ps[4][0:64, 48:64], reads=[PSB[4]],
              writes=[bdecay[par]])
        k.transp(PSB[4], ps[4][:, 96:128].bitcast(BF16)[:, 0:64],
                 xbc_v[0:64, 16, c0:c0 + Q], identb[0:64, 0:64],
                 [bxbc, bconst], pub=True)
        k.copy(bT[par], ps[4][:, 96:128].bitcast(BF16)[:, 0:64],
               reads=[PSB[4]], writes=[bbT[par]], eng="act")
        for h in range(NH):
            bk = 6 + h // 8
            dstp = ps[bk][:, (h % 8) * 64:(h % 8) * 64 + 64].bitcast(BF16)
            k.transp(PSB[bk], dstp, xbc_v[:, h, c0:c0 + Q], identb,
                     [bxbc, bconst], pub=(h % 8 == 7))
        k.copy(xT[0][:, 0:8 * Q], ps[6].bitcast(BF16)[:, 0:8 * Q],
               reads=[PSB[6]], writes=[bxT[0]], eng="act")
        k.copy(xT[0][:, 8 * Q:16 * Q], ps[7].bitcast(BF16)[:, 0:8 * Q],
               reads=[PSB[7]], writes=[bxT[0]], eng="act")
        release(6), release(7)
        if do_y:
            k.copy(Hbf[0:64, :], H[0:64, :], reads=[bH], writes=[bHbf])
            for q4 in range(4):
                for hh in range(4):
                    h = q4 * 4 + hh
                    k.mm(PSB[5], ps[5][0:1, hh * Q:(hh + 1) * Q], onesb_c,
                         dta[par][:, h * Q:(h + 1) * Q], [bdta[par], bconst],
                         last=(hh == 3))
                k.act(AF.Exp, erow[0:1, q4 * 512:(q4 + 1) * 512],
                      ps[5][0:1, 0:512], reads=[PSB[5]], writes=[berow])
                release(5)
            k.copy(ccpy[0:64, :], xbc_v[64:128, 16, c0:c0 + Q],
                   reads=[bxbc], writes=[bccpy])
            k.mm(PSB[5], ps[5][:, 0:128], xbc_v[0:64, 16, c0:c0 + Q],
                 ccpy[0:64, :], [bxbc, bccpy], last=True)
        for h in range(NH):
            hp = h % 2
            pbank, pb_ = 2 + hp, PSB[2 + hp]
            if do_y:
                k.mm(pb_, ps[pbank][:, 0:128], onesb,
                     dta[par][:, h * Q:(h + 1) * Q], [bdta[par], bconst],
                     last=False, pub=False)
                k.mm(pb_, ps[pbank][:, 0:128], dta[par][:, h * Q:(h + 1) * Q],
                     negonesb, [bdta[par], bconst], last=False, stop=True,
                     pub=True)
                k.mm(pb_, ps[pbank][0:64, 128:256], onesb_r[0:1, 0:64],
                     erow[0:1, h * Q:(h + 1) * Q], [berow, bconst],
                     last=True, pub=True)
                k.stt(lmask[hp], ps[pbank][:, 0:128], 0.0, maskneg, ALU.min,
                      ALU.add, reads=[pb_, bconst], writes=[blm[hp]])
                k.act(AF.Exp, expl[hp], lmask[hp], reads=[blm[hp]],
                      writes=[bexpl[hp]])
                k.tt(stl[hp], expl[hp], ps[5][:, 0:128], ALU.mult,
                     reads=[bexpl[hp], PSB[5]], writes=[bstl[hp]])
                k.tt(cw[hp][0:64, :], xbc_v[64:128, 16, c0:c0 + Q],
                     ps[pbank][0:64, 128:256], ALU.mult, reads=[bxbc, pb_],
                     writes=[bcw[hp]])
            k.ts1(bwt[hp][:, :], bT[par], wS[par][:, h:h + 1], ALU.mult,
                  reads=[bbT[par], bwS[par]], writes=[bbw[hp]])
            if do_y:
                qb = (h // 4) % 2
                k.mm(PSB[qb], ps[qb][:, (h % 4) * Q:(h % 4 + 1) * Q],
                     xT[0][:, h * Q:(h + 1) * Q], stl[hp],
                     [bxT[0], bstl[hp]], last=False, pub=True)
                k.mm(PSB[qb], ps[qb][:, (h % 4) * Q:(h % 4 + 1) * Q],
                     Hbf[0:64, h * Q:(h + 1) * Q], cw[hp][0:64, :],
                     [bHbf, bcw[hp]], last=False, stop=(h % 4 == 3), pub=True)
            k.mm(pb_, ps[pbank][0:64, 256:384], bwt[hp][:, :],
                 xT[0][:, h * Q:(h + 1) * Q], [bbw[hp], bxT[0]],
                 last=True, pub=True)
            k.stt(H[0:64, h * Q:(h + 1) * Q], H[0:64, h * Q:(h + 1) * Q],
                  decay[par][0:64, h:h + 1], ps[pbank][0:64, 256:384],
                  ALU.mult, ALU.add, reads=[bdecay[par], pb_], writes=[bH])
            release(pbank)
            if do_y and h % 4 == 3:
                q = h // 4
                qb = q % 2
                for hh in range(4):
                    h2 = q * 4 + hh
                    k.stt(yg_v[:, h2, c0:c0 + Q], xbc_v[:, h2, c0:c0 + Q],
                          dbc128[:, h2:h2 + 1], ps[qb][:, hh * Q:(hh + 1) * Q],
                          ALU.mult, ALU.add, reads=[bxbc, bconst, PSB[qb]],
                          writes=[byg])
                release(qb)
        if _DEBUG and gc == 0 and not do_y:
            k.dma(dbg["dbg_ws"], wS[par], reads=[bwS[par]])
            k.dma(dbg["dbg_dec"], decay[par][0:64, :], reads=[bdecay[par]])
            k.dma(dbg["dbg_cta"], ctA[par], reads=[bctA[par]])
            k.dma(dbg["dbg_cq"], cqrow, reads=[bcq])
            k.dma(dbg["dbg_h0"], H[0:64, :], reads=[bH])
            k.dma(dbg["dbg_dtvp"], dtv[0:16, :], reads=[bdtv])
            k.dma(dbg["dbg_xbcp"], xbc, reads=[bxbc])
            k.dma(dbg["dbg_bt"], bT[par], reads=[bbT[par]])
            k.dma(dbg["dbg_xt"], xT[0], reads=[bxT[0]])
        if do_y:
            for q in range(4):
                ych = yg_v[:, q * 4:q * 4 + 4, c0:c0 + Q]
                k.tt(sqs[:, 0:512], ych, ych, ALU.mult, reads=[byg],
                     writes=[bsqs], eng="dve")
                for hh in range(4):
                    k.mm(PSB[4], ps[4][0:1, 256:384], onesb_c,
                         sqs[:, hh * 128:(hh + 1) * 128], [bsqs],
                         last=False, stop=(q == 3 and hh == 3), pub=True)
            k.copy(sumsq[0:1, c0:c0 + Q], ps[4][0:1, 256:384], reads=[PSB[4]],
                   writes=[bsumsq], eng="act")
        release(4), release(5)

    def epilogue(g):
        w0 = g * GQ
        rstrow = rows[0:1, 0:512]
        k.act(AF.Ln, rstrow, sumsq, reads=[bsumsq, brows], writes=[brst],
              scale=1.0 / DI, bias=1e-5)
        k.act(AF.Exp, rstrow, rstrow, reads=[], writes=[brst], scale=-0.5)
        k.mm(PSB[2], ps[2][:, 0:512], onesf_r, rstrow, [brst], last=True)
        k.copy(rstd_bc, ps[2][:, 0:512], reads=[PSB[2]], writes=[brstdbc])
        release(2)
        for zc in range(NZC):
            zp = zc % 2
            k.dma(wzp[zp], D["wz"][zc].rearrange("p a b -> p (a b)"),
                  writes=[bwzp[zp]], eng="pool")
            for fc in range(NFC):
                k.mm(PSB[zp], ps[zp][:, 0:512], wzp[zp][:, fc * 128:(fc + 1) * 128],
                     xn_v[:, fc, 3 + w0:3 + w0 + 512], [bwzp[zp], bxn],
                     last=(fc == 7))
            k.act(AF.Sigmoid, zsig[zp], ps[zp][:, 0:512], reads=[PSB[zp]],
                  writes=[bzsig[zp]])
            release(zp)
            k.stt(yg_v[:, zc, :], yg_v[:, zc, :], rmsw[:, zc:zc + 1],
                  zsig[zp], ALU.mult, ALU.mult,
                  reads=[bconst, bzsig[zp]], writes=[byg])
        for dc in range(NDC):
            dp = dc % 2
            k.dma(wop[dp], D["wo"][dc].rearrange("p a b -> p (a b)"),
                  writes=[bwop[dp]], eng="pool")
            for kc in range(NKC):
                k.mm(PSB[dp], ps[dp][:, 0:512],
                     wop[dp][:, kc * 128:(kc + 1) * 128], yg_v[:, kc, :],
                     [bwop[dp], byg], last=(kc == 15))
            k.tt(ostage[dp], rstd_bc, ps[dp][:, 0:512], ALU.mult,
                 reads=[brstdbc, PSB[dp]], writes=[bost[dp]])
            release(dp)
            k.tt(ostage[dp], ostage[dp], ug_v[:, dc, 3:515], ALU.add,
                 reads=[bug], writes=[bost[dp]])
            dst = scratch.rearrange("(a p) n -> p a n", p=128)[:, dc, w0:w0 + 512]
            k.dma(dst, ostage[dp], reads=[bost[dp]], writes=[bscr])

    for g in range(NG):
        front_stats(D["u_prev"], g, xnp_v, bxnp, 0, do_gates=False)
        front_mm(xnp_v, bxnp, 0)
        for c in range(NCH):
            chunk(g * NCH + c, c, do_y=False)
    if _DEBUG:
        k.dma(dbg["dbg_hp"], H[0:64, :], reads=[bH])
    k.ts1(H[0:64, :], H[0:64, :], smask_s[0:64, 0:1], ALU.mult,
          reads=[bconst], writes=[bH])
    for g in range(NG):
        front_stats(D["u_own"], g, xn_v, bxn, g * GQ, do_gates=True)
        front_mm(xn_v, bxn, g * GQ)
        for c in range(NCH):
            chunk(NG * NCH + g * NCH + c, c, do_y=True)
        if _DEBUG and g == _DBG_G:
            k.dma(dbg["dbg_xbc"], xbc, reads=[bxbc])
            k.dma(dbg["dbg_dt"], dtv[0:16, :], reads=[bdtv])
            k.dma(dbg["dbg_y"], yg, reads=[byg])
        epilogue(g)
    if _DEBUG:
        k.dma(dbg["dbg_xn"], xn, reads=[bxn])
        k.dma(dbg["dbg_gates"], gates[0:8, :], reads=[bgates])
        k.dma(dbg["dbg_h"], H[0:64, :], reads=[bH])

    k.dma(acc_v, scratch.rearrange("(a p) n -> p a n", p=128), reads=[bscr],
          writes=[bacc])
    k.dma(b1c, D["b1"], writes=[bconst])
    k.dma(b2c, D["b2"], writes=[bconst], eng="pool")
    if _DEBUG:
        k.dma(dbg["dbg_scr"], scratch, reads=[bscr])
    k.memset(combtok, 0.5, writes=[bcombtok])
    k.memset(comb10[0:16, :], 0.5, writes=[bcomb10])
    for tc in range(16):
        k.transp(PSB[4], ps[4][:, 0:8], gates[0:8, tc * 128:(tc + 1) * 128],
                 identf[0:8, 0:8], [bgates], pub=True)
        k.copy(gtt, ps[4][:, 0:8], reads=[PSB[4]], writes=[bgtt])
        k.red(gsm[:, 0:1], gtt, ALU.max, reads=[bgtt], writes=[bgsm])
        k.ts1(is1t, gtt, gsm[:, 0:1], ALU.is_ge, reads=[bgtt, bgsm],
              writes=[bis1])
        k.stt(g2t, is1t, -1e30, gtt, ALU.mult, ALU.add,
              reads=[bis1, bgtt], writes=[bg2])
        k.red(gsm[:, 1:2], g2t, ALU.max, reads=[bg2], writes=[bgsm])
        k.ts1(is2t, g2t, gsm[:, 1:2], ALU.is_ge, reads=[bg2, bgsm],
              writes=[bis2])
        k.tt(gsm[:, 2:3], gsm[:, 0:1], gsm[:, 1:2], ALU.subtract,
             reads=[bgsm], writes=[bgsm])
        k.act(AF.Sigmoid, gsm[:, 3:4], gsm[:, 2:3], reads=[bgsm], writes=[bgsm])
        k.ts2(gsm[:, 4:5], gsm[:, 3:4], -1.0, 1.0, ALU.mult, ALU.add,
              reads=[bgsm], writes=[bgsm])
        k.ts1(tmp8, is1t, gsm[:, 3:4], ALU.mult, reads=[bis1, bgsm],
              writes=[btmp8])
        k.stt(combtok[:, tc * 10:tc * 10 + 8], is2t, gsm[:, 4:5], tmp8,
              ALU.mult, ALU.add, reads=[bis2, bgsm, btmp8], writes=[bcombtok])
        k.transp(PSB[4], ps[4][0:10, 128:256],
                 combtok[:, tc * 10:tc * 10 + 10], identf, [bcombtok],
                 pub=True)
        k.copy(comb10[0:10, tc * 128:(tc + 1) * 128], ps[4][0:10, 128:256],
               reads=[PSB[4]], writes=[bcomb10])
        release(4)
    for dc in range(NDC):
        for tq in range(4):
            k.mm(PSB[6], ps[6][:, 0:512], b2c[0:16, dc * 128:(dc + 1) * 128],
                 comb10[0:16, tq * 512:(tq + 1) * 512], [bconst, bcomb10],
                 last=True)
            k.tt(acc_v[:, dc, tq * 512:(tq + 1) * 512],
                 acc_v[:, dc, tq * 512:(tq + 1) * 512], ps[6][:, 0:512],
                 ALU.add, reads=[PSB[6]], writes=[bacc])
            release(6)
    for e in range(10):
        ep = e % 2
        if e < 8:
            for q4 in range(4):
                for i in range(4):
                    tc = q4 * 4 + i
                    k.transp(PSB[4], ps[4][0:1, i * 128:(i + 1) * 128],
                             combtok[:, tc * 10 + e:tc * 10 + e + 1], identf,
                             [bcombtok], pub=(i == 3))
                k.copy(combrow[0:1, q4 * 512:(q4 + 1) * 512],
                       ps[4][0:1, 0:512], reads=[PSB[4]], writes=[bcombrow])
                release(4)
            for tq in range(4):
                k.mm(PSB[5], ps[5][:, 0:512], onesb_r,
                     combrow[0:1, tq * 512:(tq + 1) * 512], [bcombrow],
                     last=True)
                k.act(AF.Copy, combbc[ep][:, tq * 512:(tq + 1) * 512],
                      ps[5][:, 0:512], reads=[PSB[5]], writes=[bcombbc[ep]])
                release(5)
        for hc in range(NHC):
            wp, bwp = w1p[hc % 3], bw1p[hc % 3]
            k.dma(wp, D["w1"][e, hc].rearrange("p a b -> p (a b)"),
                  writes=[bwp], eng="pool")
            for fc in range(NFC):
                for tq in range(4):
                    k.mm(PSB[tq], ps[tq][:, 0:512],
                         wp[:, fc * 128:(fc + 1) * 128],
                         xn_v[:, fc, 3 + tq * 512:3 + tq * 512 + 512],
                         [bwp, bxn], last=(fc == 7))
            for tq in range(4):
                hp = tq % 2
                k.act(AF.Gelu, hraw[hp], ps[tq][:, 0:512], reads=[PSB[tq]],
                      writes=[bhraw[hp]],
                      bias=b1c[:, e * 16 + hc:e * 16 + hc + 1])
                release(tq)
                if e < 8:
                    k.tt(hsb_v[:, hc, tq * 512:(tq + 1) * 512], hraw[hp],
                         combbc[ep][:, tq * 512:(tq + 1) * 512], ALU.mult,
                         reads=[bhraw[hp], bcombbc[ep]], writes=[bhsb],
                         eng="dve")
                else:
                    k.ts1(hsb_v[:, hc, tq * 512:(tq + 1) * 512], hraw[hp],
                          0.5, ALU.mult, reads=[bhraw[hp]], writes=[bhsb],
                          eng="dve")
        for dc in range(NDC):
            wp2, bp2 = w2p[dc % 2], bw2p[dc % 2]
            k.dma(wp2, D["w2"][e, dc].rearrange("p a b -> p (a b)"),
                  writes=[bp2], eng="pool")
            for hc in range(NHC):
                for tq in range(4):
                    k.mm(PSB[4 + tq], ps[4 + tq][:, 0:512],
                         wp2[:, hc * 128:(hc + 1) * 128],
                         hsb_v[:, hc, tq * 512:(tq + 1) * 512], [bp2, bhsb],
                         last=(hc == 15))
            for tq in range(4):
                k.tt(acc_v[:, dc, tq * 512:(tq + 1) * 512],
                     acc_v[:, dc, tq * 512:(tq + 1) * 512],
                     ps[4 + tq][:, 0:512],
                     ALU.add, reads=[PSB[4 + tq]], writes=[bacc])
                release(4 + tq)
    k.dma(out_d.rearrange("(a p) n -> p a n", p=128), acc_v, reads=[bacc],
          writes=[bout])
    k.S.wait("sp", bout.wtok)

    k.S.emit(nc, sems)
    return nc, list(dbg.keys())


_CACHE = {}


def kernel(**inputs):
    u = np.asarray(inputs["L0_attn_u"], np.float32)
    shared = _prep_shared(inputs)
    if "nc" not in _CACHE:
        _CACHE["nc"] = build_program()
    nc, dbg_names = _CACHE["nc"]
    in_maps = []
    for c in range(8):
        m = dict(shared)
        m.update(_prep_core(u, c))
        in_maps.append(m)
    res = run_bass_kernel_spmd(nc, in_maps, core_ids=list(range(8)))
    out = np.empty((4, 4096, 1024), np.float32)
    for c in range(8):
        b, half = divmod(c, 2)
        out[b, half * TPC:(half + 1) * TPC, :] = res.results[c]["out"].T
    if _DEBUG:
        kernel.dbg = res.results
    return out


# version 08
# speedup vs baseline: 1.1976x; 1.1976x over previous
import numpy as np
import concourse.bass as bass
import concourse.mybir as mybir
from concourse.bass_utils import run_bass_kernel_spmd
from concourse.alu_op_type import AluOpType as ALU

F32 = mybir.dt.float32
BF16 = mybir.dt.bfloat16
AF = mybir.ActivationFunctionType

DM, DI, DS, NH, HD = 1024, 2048, 64, 16, 128
NFC = 8
NCC = 17
TPC = 2048
GQ = 512
NG = 4
Q = 128
NCH = 4
NZC, NDC, NKC, NHC = 16, 8, 16, 16
SW1 = 2192

_DEBUG = 0
_DBG_G = 0
ARENA_COLS = 53200



def _prep_shared(inp):
    f = lambda x: np.ascontiguousarray(np.asarray(x, np.float32))
    d = {}
    WipT = f(inp["L0_attn_in_proj_w"]).T
    d["wsw"] = np.ascontiguousarray(
        np.concatenate([WipT[:, 2048:4096], WipT[:, 4096:4160],
                        WipT[:, 4160:4224], WipT[:, 4224:4240]], axis=1))
    d["wz"] = np.ascontiguousarray(
        WipT[:, 0:2048].reshape(8, 128, 16, 128).transpose(2, 1, 0, 3))
    d["wo"] = np.ascontiguousarray(
        f(inp["L0_attn_out_w"]).T.reshape(16, 128, 8, 128).transpose(2, 1, 0, 3))
    d["convw"] = np.ascontiguousarray(
        f(inp["L0_attn_conv_w"]).reshape(17, 128, 4).transpose(1, 0, 2)).reshape(128, 68)
    d["convb"] = np.ascontiguousarray(f(inp["L0_attn_conv_b"]).reshape(17, 128).T)
    d["dtb"] = f(inp["L0_attn_dt_bias"]).reshape(16, 1)
    d["alog16"] = f(inp["L0_attn_A_log"]).reshape(16, 1)
    d["alogr"] = f(inp["L0_attn_A_log"]).reshape(1, 16)
    d["drow"] = f(inp["L0_attn_D"]).reshape(1, 16)
    d["rmsw"] = np.ascontiguousarray(f(inp["L0_attn_rms_w"]).reshape(16, 128).T)
    d["rw"] = np.ascontiguousarray(
        f(inp["L0_ffn_router_w"]).reshape(8, 128, 8).transpose(1, 0, 2)).reshape(128, 64)
    d["rb"] = f(inp["L0_ffn_router_b"]).reshape(8, 1)
    w1c = np.concatenate([f(inp["L0_ffn_w1"]), f(inp["L0_ffn_sw1"])], axis=0)
    d["w1"] = np.ascontiguousarray(
        w1c.reshape(10, 8, 128, 16, 128).transpose(0, 3, 2, 1, 4))
    w2c = np.concatenate([f(inp["L0_ffn_w2"]), f(inp["L0_ffn_sw2"])], axis=0)
    d["w2"] = np.ascontiguousarray(
        w2c.reshape(10, 16, 128, 8, 128).transpose(0, 3, 2, 1, 4))
    b1c = np.concatenate([f(inp["L0_ffn_b1"]), f(inp["L0_ffn_sb1"])], axis=0)
    d["b1"] = np.ascontiguousarray(
        b1c.reshape(10, 16, 128).transpose(2, 0, 1)).reshape(128, 160)
    b2c = np.concatenate([f(inp["L0_ffn_b2"]), f(inp["L0_ffn_sb2"])], axis=0)
    d["b2"] = np.ascontiguousarray(
        np.concatenate([b2c, np.zeros((6, DM), np.float32)], axis=0))
    d["identf"] = np.eye(128, dtype=np.float32)
    triu = np.triu(np.ones((128, 128), np.float32))
    d["triu"] = triu
    mn = np.where(triu > 0, 0.0, -1e30).astype(np.float32)
    d["maskneg"] = np.tile(mn, (1, 4))
    d["onesf"] = np.ones((128, 128), np.float32)
    d["onescol"] = np.ones((128, 1), np.float32)
    d["cvals"] = np.tile(np.array([[1e-5, 1.0, -0.5, -1.0, 1.0 / 1024,
                                    1.0 / 2048, 0.5, 0.0]], np.float32),
                         (128, 1))
    d["negonesf"] = -np.ones((128, 128), np.float32)
    return d


def _prep_core(u, core):
    b, half = core // 2, core % 2
    z3 = np.zeros((3, DM), np.float32)
    if half == 0:
        uo = np.concatenate([z3, u[b, 0:TPC]], axis=0)
        up, mask = uo, 0.0
    else:
        uo = u[b, TPC - 3:2 * TPC]
        up = np.concatenate([z3, u[b, 0:TPC]], axis=0)
        mask = 1.0
    return {"u_own": np.ascontiguousarray(uo.T),
            "u_prev": np.ascontiguousarray(up.T),
            "smask": np.full((64, 1), mask, np.float32)}



class Sched:
    ENG = ("pe", "act", "dve", "pool", "sp")

    def __init__(self):
        self.items = {e: [] for e in self.ENG}
        self.cnt = dict.fromkeys(list(self.ENG) + ["dsp", "dpool"], 0)
        self.hwm = {e: {} for e in self.ENG}

    def op(self, eng, fn, deps=(), pub=False, dma=False):
        for tok in deps:
            if tok is None:
                continue
            s, v = tok
            if s == eng:
                continue
            if v > self.hwm[eng].get(s, 0):
                self.items[eng].append(("wait", s, v))
                self.hwm[eng][s] = v
        if dma:
            sem, amt = ("dsp" if eng == "sp" else "dpool"), 16
        elif pub:
            sem, amt = eng, 1
        else:
            sem = None
        if sem is not None:
            self.cnt[sem] += amt
            tok = (sem, self.cnt[sem])
        else:
            tok = (eng, self.cnt[eng])
        self.items[eng].append(("op", fn, sem, amt if sem else 0))
        return tok

    def wait(self, eng, tok):
        if tok is None:
            return
        s, v = tok
        if s == eng:
            return
        if v > self.hwm[eng].get(s, 0):
            self.items[eng].append(("wait", s, v))
            self.hwm[eng][s] = v

    def emit(self, nc, sems):
        S = self

        def mk(name):
            def body(h):
                for it in S.items[name]:
                    if it[0] == "wait":
                        h.wait_ge(sems[it[1]], it[2])
                    else:
                        inst = it[1](h)
                        if it[2] is not None:
                            inst.then_inc(sems[it[2]], it[3])
            return body
        with nc.Block() as block:
            block.tensor(mk("pe"))
            block.scalar(mk("act"))
            block.vector(mk("dve"))
            block.gpsimd(mk("pool"))
            block.sync(mk("sp"))


class Buf:
    __slots__ = ("wtok", "rtoks", "fresh", "widx", "nw")

    def __init__(self):
        self.wtok = None
        self.rtoks = []
        self.fresh = True
        self.widx = None
        self.nw = 9999


class Arena:
    def __init__(self, ap):
        self.ap = ap
        self.off = 0
        self.marks = {}

    def f32(self, cols, parts=128):
        c0 = self.off
        self.off += cols
        assert self.off <= ARENA_COLS, ("arena overflow", self.off)
        return self.ap[0:parts, c0:c0 + cols]

    def bf16(self, cols, parts=128):
        cf = (cols + 1) // 2
        c0 = self.off
        self.off += cf
        assert self.off <= ARENA_COLS, ("arena overflow", self.off)
        return self.ap[0:parts, c0:c0 + cf].bitcast(BF16)[0:parts, 0:cols]

    def mark(self, n):
        self.marks[n] = self.off

    def reset(self, n):
        self.off = self.marks[n]


class K:
    GUARD_DIST = 6
    GUARD_W = 64

    def __init__(self):
        self.S = Sched()
        self.seq = dict.fromkeys(Sched.ENG, 0)
        self.ndrain = 0

    def do(self, eng, fn, reads=(), writes=(), pub=True, dma=False, ow=9999):
        if eng in ("dve", "act", "pool") and not dma:
            for b in list(reads) + list(writes):
                if (b.widx is not None and b.widx[0] == eng
                        and b.nw < self.GUARD_W
                        and self.seq[eng] - b.widx[1] <= self.GUARD_DIST):
                    self.S.op(eng, lambda e: e.drain(), ())
                    self.seq[eng] += 1
                    self.ndrain += 1
                    break
        deps = []
        for b in reads:
            deps.append(b.wtok)
        for b in writes:
            deps.append(b.wtok)
            deps.extend(b.rtoks)
        tok = self.S.op(eng, fn, deps, pub=pub, dma=dma)
        self.seq[eng] += 1
        for b in writes:
            b.wtok = tok
            b.rtoks = []
            b.widx = (eng, self.seq[eng])
            b.nw = ow
        for b in reads:
            b.rtoks.append(tok)
        return tok

    def mm(self, bank, out, lhsT, rhs, reads, last, stop=None, pub=None):
        st = bank.fresh
        bank.fresh = False
        sp = last if stop is None else stop
        pb = last if pub is None else pub
        return self.do("pe",
                       lambda e, o=out, l=lhsT, r=rhs, a=st, b=sp:
                       e.matmul(o, l, r, start=a, stop=b),
                       reads=reads, writes=[bank], pub=pb)

    def transp(self, bank, out, in_, ident, reads, pub=True):
        bank.fresh = False
        return self.do("pe",
                       lambda e, o=out, i=in_, d=ident: e.transpose(o, i, d),
                       reads=reads, writes=[bank], pub=pub)

    def act(self, fn, out, in_, reads, writes, bias=0.0, scale=1.0, pub=True):
        return self.do("act",
                       lambda e, o=out, i=in_, f=fn, b=bias, s=scale:
                       e.activation(o, i, f, bias=b, scale=s),
                       reads=reads, writes=writes, pub=pub,
                       ow=int(np.prod(out.shape[1:])))

    def copy(self, out, in_, reads, writes, eng="dve", pub=True):
        if eng == "act":
            fn = lambda e, o=out, i=in_: e.copy(o, i)
        else:
            fn = lambda e, o=out, i=in_: e.tensor_copy(o, i)
        return self.do(eng, fn, reads=reads, writes=writes, pub=pub,
                       ow=int(np.prod(out.shape[1:])))

    def tt(self, out, i0, i1, op, reads, writes, eng="dve", pub=True):
        return self.do(eng, lambda e, o=out, a=i0, b=i1, p=op:
                       e.tensor_tensor(o, a, b, p),
                       reads=reads, writes=writes, pub=pub,
                       ow=int(np.prod(out.shape[1:])))

    def ts1(self, out, i0, s1, op0, reads, writes, eng="dve", pub=True):
        return self.do(eng, lambda e, o=out, a=i0, x=s1, p0=op0:
                       e.tensor_scalar(o, a, x, None, p0),
                       reads=reads, writes=writes, pub=pub,
                       ow=int(np.prod(out.shape[1:])))

    def ts2(self, out, i0, s1, s2, op0, op1, reads, writes, eng="dve", pub=True):
        return self.do(eng, lambda e, o=out, a=i0, x=s1, y=s2, p0=op0, p1=op1:
                       e.tensor_scalar(o, a, x, y, p0, p1),
                       reads=reads, writes=writes, pub=pub,
                       ow=int(np.prod(out.shape[1:])))

    def stt(self, out, i0, sc, i1, op0, op1, reads, writes, eng="dve", pub=True):
        return self.do(eng, lambda e, o=out, a=i0, x=sc, b=i1, p0=op0, p1=op1:
                       e.scalar_tensor_tensor(o, a, x, b, p0, p1),
                       reads=reads, writes=writes, pub=pub,
                       ow=int(np.prod(out.shape[1:])))

    def red(self, out, in_, op, reads, writes, eng="dve", pub=True):
        return self.do(eng, lambda e, o=out, i=in_, p=op:
                       e.tensor_reduce(o, i, mybir.AxisListType.X, p),
                       reads=reads, writes=writes, pub=pub,
                       ow=int(np.prod(out.shape[1:])))

    def dma(self, out, in_, reads=(), writes=(), eng="sp", accum=False):
        if accum:
            fn = lambda e, o=out, i=in_: e.dma_start(out=o, in_=i,
                                                     accum_op=ALU.add)
        else:
            fn = lambda e, o=out, i=in_: e.dma_start(out=o, in_=i)
        return self.do(eng, fn, reads=reads, writes=writes, dma=True)

    def memset(self, ap_, val, reads=(), writes=(), eng="pool", pub=True):
        return self.do(eng, lambda e, a=ap_, v=val: e.memset(a, v),
                       reads=reads, writes=writes, pub=pub)



def build_program():
    nc = bass.Bass("TRN2", target_bir_lowering=False)
    k = K()

    def din(name, shape):
        return nc.dram_tensor(name, list(shape), F32, kind="ExternalInput").ap()

    D = {n: din(n, s) for n, s in [
        ("u_own", (DM, 2051)), ("u_prev", (DM, 2051)), ("smask", (64, 1)),
        ("wsw", (DM, SW1)), ("wz", (16, 128, 8, 128)), ("wo", (8, 128, 16, 128)),
        ("convw", (128, 68)), ("convb", (128, 17)), ("dtb", (16, 1)),
        ("alog16", (16, 1)), ("alogr", (1, 16)), ("drow", (1, 16)),
        ("rmsw", (128, 16)), ("rw", (128, 64)), ("rb", (8, 1)),
        ("w1", (10, 16, 128, 8, 128)), ("w2", (10, 8, 128, 16, 128)),
        ("b1", (128, 160)), ("b2", (16, DM)), ("identf", (128, 128)),
        ("triu", (128, 128)), ("maskneg", (128, 512)), ("onesf", (128, 128)),
        ("negonesf", (128, 128)), ("cvals", (128, 8)),
        ("onescol", (128, 1))]}
    scratch = nc.dram_tensor("scr", [DM, TPC], F32, kind="Internal").ap()
    out_d = nc.dram_tensor("out", [DM, TPC], F32, kind="ExternalOutput").ap()
    dbg = {}
    if _DEBUG:
        def dout(name, shape, dt):
            dbg[name] = nc.dram_tensor(name, list(shape), dt,
                                       kind="ExternalOutput").ap()
        dout("dbg_xn", (128, NFC * 2051), BF16)
        dout("dbg_gates", (8, TPC), F32)
        dout("dbg_xbc", (128, NCC * GQ), BF16)
        dout("dbg_dt", (16, GQ), F32)
        dout("dbg_y", (128, NZC * GQ), BF16)
        dout("dbg_h", (64, NH * HD), F32)
        dout("dbg_hp", (64, NH * HD), F32)
        dout("dbg_scr", (DM, TPC), F32)
        dout("dbg_ws", (128, 16), F32)
        dout("dbg_dec", (64, 16), F32)
        dout("dbg_cta", (128, 16), F32)
        dout("dbg_cq", (1, 16), F32)
        dout("dbg_h0", (64, NH * HD), F32)
        dout("dbg_dtvp", (16, GQ), F32)
        dout("dbg_xbcp", (128, NCC * GQ), BF16)
        dout("dbg_bt", (128, DS), BF16)
        dout("dbg_xt", (128, NH * Q), BF16)
        dout("dbg_ps5", (128, 16), F32)
        dout("dbg_rows", (1, 1545), F32)
        dout("dbg_xn2", (128, NFC * 2051), BF16)

    ctx = []

    def enter(cm):
        v = cm.__enter__()
        ctx.append(cm)
        return v

    arena = enter(nc.sbuf_tensor("arena", [128, ARENA_COLS], F32))[:, :]
    A = Arena(arena)
    ps = [enter(nc.psum_tensor("ps%d" % i, [128, 512], F32))[:, :]
          for i in range(8)]
    sems = {n: enter(nc.semaphore("sem_" + n))
            for n in ("pe", "act", "dve", "pool", "dsp", "dpool")}

    PSB = [Buf() for _ in range(8)]

    def release(i):
        PSB[i].fresh = True

    identf = A.f32(128)
    identb = A.bf16(128)
    triu_b = A.bf16(128)
    maskneg4 = A.f32(512)
    onesb = A.bf16(128)
    negonesb = A.bf16(128)
    onesf_c = A.f32(1)
    onesb_c = A.bf16(2)[:, 0:1]
    onesf_r = A.f32(128, parts=1)
    onesb_r = A.bf16(128, parts=1)
    cvals_s = A.f32(8)
    xn = A.bf16(NFC * 2051)
    gates = A.f32(TPC, parts=8)
    A.mark("perm")

    convw = A.f32(68)
    convb = A.f32(17)
    dtb = A.f32(1, parts=16)
    alog16 = A.f32(1, parts=16)
    alogr = A.f32(16, parts=1)
    abc128 = A.f32(16)
    dbc128 = A.f32(16)
    drow_s = A.f32(16, parts=1)
    rmsw = A.f32(16)
    rw = A.f32(64)
    rb = A.f32(1, parts=8)
    csrw = A.f32(1, parts=8)
    smask_s = A.f32(1, parts=64)
    H = A.f32(NH * HD, parts=64)
    Hbf = A.bf16(NH * HD, parts=64)
    sumsq = A.f32(GQ, parts=1)
    erow = A.bf16(NH * Q, parts=1)
    cqrow = A.f32(16, parts=1)
    dtsum = A.f32(1, parts=16)
    ug = A.f32(NFC * 515)
    t1n = A.f32(515)
    sqln = A.bf16(515)
    rows = A.f32(1545, parts=1)
    wsw = A.bf16(NFC * SW1)
    preconv = [A.f32(515)]
    convacc = A.f32(512)
    xbc = A.bf16(NCC * GQ)
    dtg = A.f32(515, parts=16)
    dtv = A.f32(GQ, parts=16)
    dtT = [A.f32(16), A.f32(16)]
    dta = [A.bf16(NH * Q), A.bf16(NH * Q)]
    ctA = [A.f32(16), A.f32(16)]
    wS = [A.f32(16), A.f32(16)]
    decay = [A.f32(16, parts=64), A.f32(16, parts=64)]
    bT = [A.bf16(DS), A.bf16(DS)]
    bw4 = [A.bf16(4 * DS), A.bf16(4 * DS)]
    xT = [A.bf16(NH * Q)]
    lmask4 = [A.f32(512), A.f32(512)]
    expl4 = lmask4
    stl4 = [A.bf16(512), A.bf16(512)]
    cw4 = [A.bf16(512, parts=64), A.bf16(512, parts=64)]
    yg = A.bf16(NZC * GQ)
    sqs = A.bf16(4 * Q)
    ccpy = A.bf16(Q, parts=64)
    zsig = [A.bf16(GQ), A.bf16(GQ)]
    wzp = [A.bf16(NFC * 128), A.bf16(NFC * 128)]
    rstd_bc = A.f32(GQ)
    ostage = [A.f32(GQ), A.f32(GQ)]
    A.mark("ap_shared")
    wop = [A.bf16(NKC * 128), A.bf16(NKC * 128)]
    A.mark("ap_end")

    A.reset("perm")
    acc = A.f32(NDC * TPC)
    hsb = A.bf16(NHC * TPC)
    w1p = [A.bf16(NFC * 128) for _ in range(3)]
    w2p = [A.bf16(NKC * 128) for _ in range(2)]
    hraw = [A.bf16(GQ), A.bf16(GQ)]
    combtok = A.f32(160)
    comb10 = A.bf16(TPC, parts=16)
    combrow = A.bf16(TPC, parts=1)
    combbc = [A.bf16(TPC), A.bf16(TPC)]
    b1c = A.f32(160)
    b2c = A.bf16(DM, parts=16)
    gtt = A.f32(8)
    gsm = A.f32(8)
    is1t = A.f32(8)
    g2t = A.f32(8)
    is2t = A.f32(8)
    tmp8 = A.f32(8)
    f_high = A.off
    assert max(A.marks["ap_end"], f_high) <= ARENA_COLS

    xn_v = xn.rearrange("p (a b) -> p a b", a=NFC)
    ug_v = ug.rearrange("p (a b) -> p a b", a=NFC)
    wsw_v = wsw.rearrange("p (a b) -> p a b", a=NFC)
    rw_v = rw.rearrange("p (a b) -> p a b", a=NFC)
    xbc_v = xbc.rearrange("p (a b) -> p a b", a=NCC)
    yg_v = yg.rearrange("p (a b) -> p a b", a=NZC)
    acc_v = acc.rearrange("p (a b) -> p a b", a=NDC)
    hsb_v = hsb.rearrange("p (a b) -> p a b", a=NHC)

    NB = lambda: Buf()
    bconst = NB()
    bug, bxn, bgates, bgtmp = NB(), NB(), NB(), NB()
    brows, bsqln, bt1n, brst = NB(), NB(), NB(), NB()
    bwsw = NB()
    bpre = [NB()]
    bconvacc = NB()
    bxbc, bdtg, bdtv = NB(), NB(), NB()
    bdtT = [NB(), NB()]
    bdta = [NB(), NB()]
    bctA = [NB(), NB()]
    bwS = [NB(), NB()]
    bdecay = [NB(), NB()]
    bbT = [NB(), NB()]
    bbw = [NB(), NB()]
    bxT = [NB()]
    blm = [NB(), NB()]
    bexpl = [NB(), NB()]
    bstl = [NB(), NB()]
    bcw = [NB(), NB()]
    byg, bsqs = NB(), NB()
    bccpy = NB()
    bzsig = [NB(), NB()]
    bwzp = [NB(), NB()]
    bwop = [NB(), NB()]
    brstdbc = NB()
    bost = [NB(), NB()]
    bH, bHbf = NB(), NB()
    bsumsq, berow, bcq, bdtsum = NB(), NB(), NB(), NB()
    bscr, bout = NB(), NB()
    bacc, bhsb = NB(), NB()
    bw1p = [NB(), NB(), NB()]
    bw2p = [NB(), NB()]
    bhraw = [NB(), NB()]
    bcombtok, bcomb10, bcombrow = NB(), NB(), NB()
    bcombbc = [NB(), NB()]
    bgtt, bgsm = NB(), NB()
    bis1, bg2, bis2, btmp8 = NB(), NB(), NB(), NB()

    for dst, src in [(identf, "identf"), (maskneg4, "maskneg"), (convw, "convw"),
                     (convb, "convb"), (dtb, "dtb"), (alog16, "alog16"),
                     (alogr, "alogr"), (rmsw, "rmsw"), (rw, "rw"), (rb, "rb"),
                     (drow_s, "drow"), (smask_s, "smask")]:
        k.dma(dst, D[src], writes=[bconst])
    for dst, src in [(identb, "identf"), (triu_b, "triu"), (onesb, "onesf"),
                     (negonesb, "negonesf")]:
        k.dma(dst, D[src], writes=[bconst], eng="pool")
    k.dma(onesf_c, D["onescol"], writes=[bconst], eng="pool")
    k.dma(onesb_c, D["onescol"], writes=[bconst], eng="pool")
    k.dma(onesf_r, D["onesf"][0:1, :], writes=[bconst], eng="pool")
    k.dma(onesb_r, D["onesf"][0:1, :], writes=[bconst], eng="pool")
    for fc in range(NFC):
        k.dma(wsw_v[:, fc, :], D["wsw"][fc * 128:(fc + 1) * 128, :],
              writes=[bwsw], eng="pool")
    k.memset(H[0:64, :], 0.0, writes=[bH])
    tcv = k.dma(cvals_s, D["cvals"], writes=[bconst])
    for i, v in enumerate([1e-5, 1.0, -0.5, -1.0, 1.0 / 1024, 1.0 / 2048,
                           0.5, 0.0]):
        nc.const_aps.register(F32, v, cvals_s[:, i:i + 1])
    for e in ("pe", "act", "dve", "pool"):
        k.S.wait(e, tcv)

    k.act(AF.Exp, alog16, alog16, reads=[bconst], writes=[bconst])
    k.ts1(alog16, alog16, -1.0, ALU.mult, reads=[], writes=[bconst])
    k.act(AF.Exp, alogr, alogr, reads=[bconst], writes=[bconst])
    k.ts1(alogr, alogr, -1.0, ALU.mult, reads=[], writes=[bconst])
    k.mm(PSB[6], ps[6][:, 0:16], onesf_r, alogr, [bconst], last=True)
    k.copy(abc128, ps[6][:, 0:16], reads=[PSB[6]], writes=[bconst])
    release(6)
    for fc in range(NFC):
        k.mm(PSB[6], ps[6][0:8, 16:17], rw_v[:, fc, :], onesf_c, [bconst],
             last=(fc == 7))
    k.copy(csrw[0:8, 0:1], ps[6][0:8, 16:17], reads=[PSB[6]], writes=[bconst])
    release(6)
    k.mm(PSB[7], ps[7][:, 0:16], onesf_r, drow_s, [bconst], last=True)
    k.copy(dbc128, ps[7][:, 0:16], reads=[PSB[7]], writes=[bconst])
    release(7)

    def front_stats(u_dram, g, xdst, bxd, xw0, do_gates):
        w0 = g * GQ
        uin = u_dram.rearrange("(a p) n -> p a n", p=128)[:, :, w0:w0 + 515]
        k.dma(ug_v, uin, writes=[bug])
        for fc in range(NFC):
            k.act(AF.Square, sqln, ug_v[:, fc, :], reads=[bug], writes=[bsqln])
            k.mm(PSB[4], ps[4][0:1, 0:512], onesf_c, ug_v[:, fc, 0:512],
                 [bug, bconst], last=(fc == 7), pub=True)
            k.mm(PSB[5], ps[5][0:1, 0:3], onesf_c, ug_v[:, fc, 512:515],
                 [bug, bconst], last=False, stop=(fc == 7), pub=True)
            k.mm(PSB[6], ps[6][0:1, 0:512], onesb_c, sqln[:, 0:512],
                 [bsqln, bconst], last=(fc == 7), pub=True)
            k.mm(PSB[5], ps[5][0:1, 4:7], onesb_c, sqln[:, 512:515],
                 [bsqln, bconst], last=(fc == 7), pub=True)
        mean_r = rows[0:1, 0:515]
        msq_r = rows[0:1, 515:1030]
        rstd_r = rows[0:1, 1030:1545]
        k.act(AF.Copy, mean_r[0:1, 0:512], ps[4][0:1, 0:512], reads=[PSB[4]],
              writes=[brows], scale=1.0 / DM)
        k.act(AF.Copy, mean_r[0:1, 512:515], ps[5][0:1, 0:3], reads=[PSB[5]],
              writes=[brows], scale=1.0 / DM)
        k.act(AF.Copy, msq_r[0:1, 0:512], ps[6][0:1, 0:512], reads=[PSB[6]],
              writes=[brows], scale=1.0 / DM)
        k.act(AF.Copy, msq_r[0:1, 512:515], ps[5][0:1, 4:7], reads=[PSB[5]],
              writes=[brows], scale=1.0 / DM)
        release(4), release(5), release(6)
        k.tt(rstd_r, mean_r, mean_r, ALU.mult, reads=[brows], writes=[brows])
        k.tt(rstd_r, msq_r, rstd_r, ALU.subtract, reads=[], writes=[brows])
        k.act(AF.Ln, rstd_r, rstd_r, reads=[brows], writes=[brows], bias=1e-5)
        k.act(AF.Exp, rstd_r, rstd_r, reads=[], writes=[brows], scale=-0.5)
        k.tt(mean_r, mean_r, rstd_r, ALU.mult, reads=[brows], writes=[brows])
        mrs_r = mean_r
        k.mm(PSB[4], ps[4][:, 0:512], onesf_r, rstd_r[0:1, 0:512], [brows],
             last=True)
        k.mm(PSB[5], ps[5][:, 8:11], onesf_r, rstd_r[0:1, 512:515], [brows],
             last=False, stop=True)
        k.mm(PSB[6], ps[6][:, 0:512], onesf_r, mrs_r[0:1, 0:512], [brows],
             last=True)
        k.mm(PSB[5], ps[5][:, 12:15], onesf_r, mrs_r[0:1, 512:515], [brows],
             last=True)
        if _DEBUG and do_gates and g == 3:
            k.copy(ostage[1][:, 0:16], ps[5][:, 0:16], reads=[PSB[5]],
                   writes=[bost[1]])
            k.dma(dbg["dbg_ps5"], ostage[1][:, 0:16], reads=[bost[1]])
            k.dma(dbg["dbg_rows"], rows[0:1, :], reads=[brows])
        for fc in range(NFC):
            k.tt(t1n[:, 0:512], ug_v[:, fc, 0:512], ps[4][:, 0:512], ALU.mult,
                 reads=[bug, PSB[4]], writes=[bt1n])
            k.tt(xdst[:, fc, xw0:xw0 + 512], t1n[:, 0:512], ps[6][:, 0:512],
                 ALU.subtract, reads=[bt1n, PSB[6]], writes=[bxd])
            k.tt(t1n[:, 512:515], ug_v[:, fc, 512:515], ps[5][:, 8:11],
                 ALU.mult, reads=[bug, PSB[5]], writes=[bt1n])
            k.tt(xdst[:, fc, xw0 + 512:xw0 + 515], t1n[:, 512:515],
                 ps[5][:, 12:15], ALU.subtract, reads=[bt1n, PSB[5]],
                 writes=[bxd])
        if _DEBUG and do_gates and g == 3:
            k.dma(dbg["dbg_xn2"], xn, reads=[bxn])
        if do_gates:
            for fc in range(NFC):
                k.mm(PSB[7], ps[7][0:8, 0:512], rw_v[:, fc, :],
                     ug_v[:, fc, 3:515], [bug, bconst], last=(fc == 7))
            g0 = gates[0:8, w0:w0 + 512]
            k.copy(g0, ps[7][0:8, 0:512], reads=[PSB[7]], writes=[bgates])
            release(7)
            k.mm(PSB[7], ps[7][0:8, 0:512], onesf_r[0:1, 0:8],
                 rstd_r[0:1, 3:515], [brows], last=True)
            k.tt(g0, g0, ps[7][0:8, 0:512], ALU.mult, reads=[PSB[7]],
                 writes=[bgates])
            release(7)
            k.mm(PSB[7], ps[7][0:8, 0:512], onesf_r[0:1, 0:8],
                 mrs_r[0:1, 3:515], [brows], last=True)
            gtmp = ostage[0][0:8, :]
            k.ts1(gtmp, ps[7][0:8, 0:512], csrw[0:8, 0:1], ALU.mult,
                  reads=[PSB[7], bconst], writes=[bost[0]])
            release(7)
            k.tt(g0, g0, gtmp, ALU.subtract, reads=[bost[0]],
                 writes=[bgates])
            k.ts1(g0, g0, rb[0:8, 0:1], ALU.add, reads=[bconst],
                  writes=[bgates])
        release(4), release(5), release(6)

    def front_mm(xdst, bxd, xw0):
        for cc in range(NCC + 1):
            M = 128 if cc < NCC else 16
            pa, pb = ps[cc % 2], ps[2 + cc % 2]
            ba, bb = PSB[cc % 2], PSB[2 + cc % 2]
            for fc in range(NFC):
                lhs = wsw_v[:, fc, cc * 128:cc * 128 + M]
                k.mm(ba, pa[0:M, 0:512], lhs, xdst[:, fc, xw0:xw0 + 512],
                     [bxd, bwsw], last=(fc == 7))
                k.mm(bb, pb[0:M, 0:3], lhs, xdst[:, fc, xw0 + 512:xw0 + 515],
                     [bxd, bwsw], last=(fc == 7))
            if cc < NCC:
                pc, bpc = preconv[0], bpre[0]
                k.act(AF.Copy, pc[:, 0:512], pa[0:128, 0:512], reads=[ba],
                      writes=[bpc])
                k.act(AF.Copy, pc[:, 512:515], pb[0:128, 0:3], reads=[bb],
                      writes=[bpc])
                release(cc % 2), release(2 + cc % 2)
                ceng = "dve"
                k.ts1(convacc, pc[:, 0:512], convw[:, 4 * cc:4 * cc + 1],
                      ALU.mult, reads=[bpc, bconst], writes=[bconvacc],
                      eng=ceng)
                for t in range(1, 4):
                    k.stt(convacc, pc[:, t:512 + t],
                          convw[:, 4 * cc + t:4 * cc + t + 1], convacc,
                          ALU.mult, ALU.add, reads=[bpc, bconst],
                          writes=[bconvacc], eng=ceng)
                k.act(AF.Silu, xbc_v[:, cc, :], convacc, reads=[bconvacc],
                      writes=[bxbc], bias=convb[:, cc:cc + 1])
            else:
                k.act(AF.Copy, dtg[0:16, 0:512], pa[0:16, 0:512], reads=[ba],
                      writes=[bdtg])
                k.act(AF.Copy, dtg[0:16, 512:515], pb[0:16, 0:3], reads=[bb],
                      writes=[bdtg])
                release(cc % 2), release(2 + cc % 2)
                k.ts2(dtv[0:16, :], dtg[0:16, 3:515], dtb[0:16, 0:1], 0.0,
                      ALU.add, ALU.max, reads=[bdtg, bconst], writes=[bdtv])
                k.act(AF.Abs, dtg[0:16, 3:515], dtg[0:16, 3:515], reads=[bdtg],
                      writes=[bdtg], bias=dtb[0:16, 0:1])
                k.act(AF.Exp, dtg[0:16, 3:515], dtg[0:16, 3:515], reads=[],
                      writes=[bdtg], scale=-1.0)
                k.act(AF.Ln, dtg[0:16, 3:515], dtg[0:16, 3:515], reads=[],
                      writes=[bdtg], bias=1.0)
                k.tt(dtv[0:16, :], dtv[0:16, :], dtg[0:16, 3:515], ALU.add,
                     reads=[bdtg], writes=[bdtv])

    def chunk(gc, c, do_y):
        par = gc % 2
        c0 = c * Q
        dtc = dtv[0:16, c0:c0 + Q]
        k.transp(PSB[4], ps[4][:, 0:16], dtc, identf[0:16, 0:16],
                 [bdtv, bconst], pub=True)
        k.copy(dtT[par], ps[4][:, 0:16], reads=[PSB[4]], writes=[bdtT[par]], eng="act")
        for h in range(NH):
            k.ts2(dta[par][:, h * Q:(h + 1) * Q], triu_b, dtT[par][:, h:h + 1],
                  abc128[:, h:h + 1], ALU.mult, ALU.mult,
                  reads=[bconst, bdtT[par]], writes=[bdta[par]],
                  pub=(h % 4 == 3))
        for h in range(NH):
            k.mm(PSB[4], ps[4][0:1, 64 + h:65 + h], onesb_c,
                 dta[par][:, h * Q + Q - 1:h * Q + Q], [bdta[par], bconst],
                 last=(h == 15), pub=True)
        k.copy(cqrow, ps[4][0:1, 64:80], reads=[PSB[4]], writes=[bcq], eng="act")
        for h in range(NH):
            k.mm(PSB[4], ps[4][:, 16 + h:17 + h],
                 dta[par][:, h * Q:(h + 1) * Q], onesb_c, [bdta[par], bconst],
                 last=(h == 15))
        k.copy(ctA[par], ps[4][:, 16:32], reads=[PSB[4]], writes=[bctA[par]], eng="act")
        k.mm(PSB[4], ps[4][:, 32:48], onesf_r, cqrow, [bcq], last=True)
        k.stt(wS[par], ctA[par], -1.0, ps[4][:, 32:48], ALU.mult, ALU.add,
              reads=[bctA[par], PSB[4]], writes=[bwS[par]])
        k.act(AF.Exp, wS[par], wS[par], reads=[bwS[par]], writes=[bwS[par]])
        k.mm(PSB[4], ps[4][0:64, 48:64], onesf_r[0:1, 0:64], cqrow, [bcq],
             last=True)
        k.act(AF.Exp, decay[par][0:64, :], ps[4][0:64, 48:64], reads=[PSB[4]],
              writes=[bdecay[par]])
        k.transp(PSB[4], ps[4][:, 96:128].bitcast(BF16)[:, 0:64],
                 xbc_v[0:64, 16, c0:c0 + Q], identb[0:64, 0:64],
                 [bxbc, bconst], pub=True)
        k.copy(bT[par], ps[4][:, 96:128].bitcast(BF16)[:, 0:64],
               reads=[PSB[4]], writes=[bbT[par]], eng="act")
        for h in range(NH):
            bk = 6 + h // 8
            dstp = ps[bk][:, (h % 8) * 64:(h % 8) * 64 + 64].bitcast(BF16)
            k.transp(PSB[bk], dstp, xbc_v[:, h, c0:c0 + Q], identb,
                     [bxbc, bconst], pub=(h % 8 == 7))
        k.copy(xT[0][:, 0:8 * Q], ps[6].bitcast(BF16)[:, 0:8 * Q],
               reads=[PSB[6]], writes=[bxT[0]], eng="act")
        k.copy(xT[0][:, 8 * Q:16 * Q], ps[7].bitcast(BF16)[:, 0:8 * Q],
               reads=[PSB[7]], writes=[bxT[0]], eng="act")
        release(6), release(7)
        if do_y:
            k.copy(Hbf[0:64, :], H[0:64, :], reads=[bH], writes=[bHbf])
            for q4 in range(4):
                for hh in range(4):
                    h = q4 * 4 + hh
                    k.mm(PSB[5], ps[5][0:1, hh * Q:(hh + 1) * Q], onesb_c,
                         dta[par][:, h * Q:(h + 1) * Q], [bdta[par], bconst],
                         last=(hh == 3))
                k.act(AF.Exp, erow[0:1, q4 * 512:(q4 + 1) * 512],
                      ps[5][0:1, 0:512], reads=[PSB[5]], writes=[berow])
                release(5)
            k.copy(ccpy[0:64, :], xbc_v[64:128, 16, c0:c0 + Q],
                   reads=[bxbc], writes=[bccpy])
            for rep in range(4):
                k.mm(PSB[5], ps[5][:, rep * 128:(rep + 1) * 128],
                     xbc_v[0:64, 16, c0:c0 + Q], ccpy[0:64, :],
                     [bxbc, bccpy], last=(rep == 3), pub=True)
        def stageA(q):
            qp = q % 2
            dbk, ebk = ((2, 3) if qp == 0 else (6, 7))
            if do_y:
                for hh in range(4):
                    h = q * 4 + hh
                    k.mm(PSB[dbk], ps[dbk][:, hh * Q:(hh + 1) * Q], onesb,
                         dta[par][:, h * Q:(h + 1) * Q], [bdta[par], bconst],
                         last=False, pub=False)
                    k.mm(PSB[dbk], ps[dbk][:, hh * Q:(hh + 1) * Q],
                         dta[par][:, h * Q:(h + 1) * Q], negonesb,
                         [bdta[par], bconst], last=False, stop=(hh == 3),
                         pub=(hh == 3))
                for hh in range(4):
                    h = q * 4 + hh
                    k.mm(PSB[ebk], ps[ebk][0:64, hh * Q:(hh + 1) * Q],
                         onesb_r[0:1, 0:64], erow[0:1, h * Q:(h + 1) * Q],
                         [berow, bconst], last=(hh == 3), pub=(hh == 3))
                k.stt(lmask4[qp], ps[dbk][:, 0:512], 0.0, maskneg4, ALU.min,
                      ALU.add, reads=[PSB[dbk], bconst], writes=[blm[qp]])
                k.act(AF.Exp, expl4[qp], lmask4[qp], reads=[],
                      writes=[blm[qp]])
                k.tt(stl4[qp], expl4[qp], ps[5][:, 0:512], ALU.mult,
                     reads=[blm[qp], PSB[5]], writes=[bstl[qp]])
                for hh in range(4):
                    k.tt(cw4[qp][0:64, hh * Q:(hh + 1) * Q],
                         xbc_v[64:128, 16, c0:c0 + Q],
                         ps[ebk][0:64, hh * Q:(hh + 1) * Q], ALU.mult,
                         reads=[bxbc, PSB[ebk]], writes=[bcw[qp]])
            for hh in range(4):
                h = q * 4 + hh
                k.ts1(bw4[qp][:, hh * DS:(hh + 1) * DS], bT[par],
                      wS[par][:, h:h + 1], ALU.mult,
                      reads=[bbT[par], bwS[par]], writes=[bbw[qp]])

        def stageB(q):
            qp = q % 2
            dbk, ebk = ((2, 3) if qp == 0 else (6, 7))
            qb = q % 2
            release(dbk)
            for hh in range(4):
                h = q * 4 + hh
                k.mm(PSB[dbk], ps[dbk][0:64, hh * Q:(hh + 1) * Q],
                     bw4[qp][:, hh * DS:(hh + 1) * DS],
                     xT[0][:, h * Q:(h + 1) * Q], [bbw[qp], bxT[0]],
                     last=(hh == 3), pub=(hh == 3))
            if do_y:
                for hh in range(4):
                    h = q * 4 + hh
                    k.mm(PSB[qb], ps[qb][:, hh * Q:(hh + 1) * Q],
                         xT[0][:, h * Q:(h + 1) * Q],
                         stl4[qp][:, hh * Q:(hh + 1) * Q],
                         [bxT[0], bstl[qp]], last=False, pub=(hh == 3))
                    k.mm(PSB[qb], ps[qb][:, hh * Q:(hh + 1) * Q],
                         Hbf[0:64, h * Q:(h + 1) * Q],
                         cw4[qp][0:64, hh * Q:(hh + 1) * Q],
                         [bHbf, bcw[qp]], last=False, stop=(hh == 3),
                         pub=(hh == 3))
            for hh in range(4):
                h = q * 4 + hh
                k.stt(H[0:64, h * Q:(h + 1) * Q], H[0:64, h * Q:(h + 1) * Q],
                      decay[par][0:64, h:h + 1],
                      ps[dbk][0:64, hh * Q:(hh + 1) * Q],
                      ALU.mult, ALU.add, reads=[bdecay[par], PSB[dbk]],
                      writes=[bH])
            release(dbk), release(ebk)
            if do_y:
                for hh in range(4):
                    h2 = q * 4 + hh
                    k.stt(yg_v[:, h2, c0:c0 + Q], xbc_v[:, h2, c0:c0 + Q],
                          dbc128[:, h2:h2 + 1], ps[qb][:, hh * Q:(hh + 1) * Q],
                          ALU.mult, ALU.add, reads=[bxbc, bconst, PSB[qb]],
                          writes=[byg])
                release(qb)
                ych = yg_v[:, q * 4:q * 4 + 4, c0:c0 + Q]
                k.tt(sqs[:, 0:512], ych, ych, ALU.mult, reads=[byg],
                     writes=[bsqs], eng="dve")
                for hh in range(4):
                    k.mm(PSB[4], ps[4][0:1, 256:384], onesb_c,
                         sqs[:, hh * 128:(hh + 1) * 128], [bsqs],
                         last=False, stop=(q == 3 and hh == 3), pub=True)

        stageA(0)
        for q in range(4):
            if q + 1 < 4:
                stageA(q + 1)
            stageB(q)
        if _DEBUG and gc == 0 and not do_y:
            k.dma(dbg["dbg_ws"], wS[par], reads=[bwS[par]])
            k.dma(dbg["dbg_dec"], decay[par][0:64, :], reads=[bdecay[par]])
            k.dma(dbg["dbg_cta"], ctA[par], reads=[bctA[par]])
            k.dma(dbg["dbg_cq"], cqrow, reads=[bcq])
            k.dma(dbg["dbg_h0"], H[0:64, :], reads=[bH])
            k.dma(dbg["dbg_dtvp"], dtv[0:16, :], reads=[bdtv])
            k.dma(dbg["dbg_xbcp"], xbc, reads=[bxbc])
            k.dma(dbg["dbg_bt"], bT[par], reads=[bbT[par]])
            k.dma(dbg["dbg_xt"], xT[0], reads=[bxT[0]])
        if do_y:
            k.copy(sumsq[0:1, c0:c0 + Q], ps[4][0:1, 256:384], reads=[PSB[4]],
                   writes=[bsumsq], eng="act")
        release(4), release(5)

    def epilogue(g):
        w0 = g * GQ
        rstrow = rows[0:1, 0:512]
        k.act(AF.Ln, rstrow, sumsq, reads=[bsumsq, brows], writes=[brst],
              scale=1.0 / DI, bias=1e-5)
        k.act(AF.Exp, rstrow, rstrow, reads=[], writes=[brst], scale=-0.5)
        k.mm(PSB[2], ps[2][:, 0:512], onesf_r, rstrow, [brst], last=True)
        k.copy(rstd_bc, ps[2][:, 0:512], reads=[PSB[2]], writes=[brstdbc])
        release(2)
        for zc in range(NZC):
            zp = zc % 2
            k.dma(wzp[zp], D["wz"][zc].rearrange("p a b -> p (a b)"),
                  writes=[bwzp[zp]], eng="pool")
            for fc in range(NFC):
                k.mm(PSB[zp], ps[zp][:, 0:512], wzp[zp][:, fc * 128:(fc + 1) * 128],
                     xn_v[:, fc, 3 + w0:3 + w0 + 512], [bwzp[zp], bxn],
                     last=(fc == 7))
            k.act(AF.Sigmoid, zsig[zp], ps[zp][:, 0:512], reads=[PSB[zp]],
                  writes=[bzsig[zp]])
            release(zp)
            k.stt(yg_v[:, zc, :], yg_v[:, zc, :], rmsw[:, zc:zc + 1],
                  zsig[zp], ALU.mult, ALU.mult,
                  reads=[bconst, bzsig[zp]], writes=[byg])
        for dc in range(NDC):
            dp = dc % 2
            k.dma(wop[dp], D["wo"][dc].rearrange("p a b -> p (a b)"),
                  writes=[bwop[dp]], eng="pool")
            for kc in range(NKC):
                k.mm(PSB[dp], ps[dp][:, 0:512],
                     wop[dp][:, kc * 128:(kc + 1) * 128], yg_v[:, kc, :],
                     [bwop[dp], byg], last=(kc == 15))
            k.tt(ostage[dp], rstd_bc, ps[dp][:, 0:512], ALU.mult,
                 reads=[brstdbc, PSB[dp]], writes=[bost[dp]])
            release(dp)
            k.tt(ostage[dp], ostage[dp], ug_v[:, dc, 3:515], ALU.add,
                 reads=[bug], writes=[bost[dp]])
            dst = scratch.rearrange("(a p) n -> p a n", p=128)[:, dc, w0:w0 + 512]
            k.dma(dst, ostage[dp], reads=[bost[dp]], writes=[bscr])

    for g in range(NG):
        front_stats(D["u_prev"], g, xn_v, bxn, 0, do_gates=False)
        front_mm(xn_v, bxn, 0)
        for c in range(NCH):
            chunk(g * NCH + c, c, do_y=False)
    if _DEBUG:
        k.dma(dbg["dbg_hp"], H[0:64, :], reads=[bH])
    k.ts1(H[0:64, :], H[0:64, :], smask_s[0:64, 0:1], ALU.mult,
          reads=[bconst], writes=[bH])
    for g in range(NG):
        front_stats(D["u_own"], g, xn_v, bxn, g * GQ, do_gates=True)
        front_mm(xn_v, bxn, g * GQ)
        for c in range(NCH):
            chunk(NG * NCH + g * NCH + c, c, do_y=True)
        if _DEBUG and g == _DBG_G:
            k.dma(dbg["dbg_xbc"], xbc, reads=[bxbc])
            k.dma(dbg["dbg_dt"], dtv[0:16, :], reads=[bdtv])
            k.dma(dbg["dbg_y"], yg, reads=[byg])
        epilogue(g)
    if _DEBUG:
        k.dma(dbg["dbg_xn"], xn, reads=[bxn])
        k.dma(dbg["dbg_gates"], gates[0:8, :], reads=[bgates])
        k.dma(dbg["dbg_h"], H[0:64, :], reads=[bH])

    k.dma(acc_v, scratch.rearrange("(a p) n -> p a n", p=128), reads=[bscr],
          writes=[bacc])
    k.dma(b1c, D["b1"], writes=[bconst])
    k.dma(b2c, D["b2"], writes=[bconst], eng="pool")
    if _DEBUG:
        k.dma(dbg["dbg_scr"], scratch, reads=[bscr])
    k.memset(combtok, 0.5, writes=[bcombtok])
    k.memset(comb10[0:16, :], 0.5, writes=[bcomb10])
    for tc in range(16):
        k.transp(PSB[4], ps[4][:, 0:8], gates[0:8, tc * 128:(tc + 1) * 128],
                 identf[0:8, 0:8], [bgates], pub=True)
        k.copy(gtt, ps[4][:, 0:8], reads=[PSB[4]], writes=[bgtt])
        k.red(gsm[:, 0:1], gtt, ALU.max, reads=[bgtt], writes=[bgsm])
        k.ts1(is1t, gtt, gsm[:, 0:1], ALU.is_ge, reads=[bgtt, bgsm],
              writes=[bis1])
        k.stt(g2t, is1t, -1e30, gtt, ALU.mult, ALU.add,
              reads=[bis1, bgtt], writes=[bg2])
        k.red(gsm[:, 1:2], g2t, ALU.max, reads=[bg2], writes=[bgsm])
        k.ts1(is2t, g2t, gsm[:, 1:2], ALU.is_ge, reads=[bg2, bgsm],
              writes=[bis2])
        k.tt(gsm[:, 2:3], gsm[:, 0:1], gsm[:, 1:2], ALU.subtract,
             reads=[bgsm], writes=[bgsm])
        k.act(AF.Sigmoid, gsm[:, 3:4], gsm[:, 2:3], reads=[bgsm], writes=[bgsm])
        k.ts2(gsm[:, 4:5], gsm[:, 3:4], -1.0, 1.0, ALU.mult, ALU.add,
              reads=[bgsm], writes=[bgsm])
        k.ts1(tmp8, is1t, gsm[:, 3:4], ALU.mult, reads=[bis1, bgsm],
              writes=[btmp8])
        k.stt(combtok[:, tc * 10:tc * 10 + 8], is2t, gsm[:, 4:5], tmp8,
              ALU.mult, ALU.add, reads=[bis2, bgsm, btmp8], writes=[bcombtok])
        k.transp(PSB[4], ps[4][0:10, 128:256],
                 combtok[:, tc * 10:tc * 10 + 10], identf, [bcombtok],
                 pub=True)
        k.copy(comb10[0:10, tc * 128:(tc + 1) * 128], ps[4][0:10, 128:256],
               reads=[PSB[4]], writes=[bcomb10])
        release(4)
    for dc in range(NDC):
        for tq in range(4):
            k.mm(PSB[6], ps[6][:, 0:512], b2c[0:16, dc * 128:(dc + 1) * 128],
                 comb10[0:16, tq * 512:(tq + 1) * 512], [bconst, bcomb10],
                 last=True)
            k.tt(acc_v[:, dc, tq * 512:(tq + 1) * 512],
                 acc_v[:, dc, tq * 512:(tq + 1) * 512], ps[6][:, 0:512],
                 ALU.add, reads=[PSB[6]], writes=[bacc])
            release(6)
    for e in range(10):
        ep = e % 2
        if e < 8:
            for q4 in range(4):
                for i in range(4):
                    tc = q4 * 4 + i
                    k.transp(PSB[4], ps[4][0:1, i * 128:(i + 1) * 128],
                             combtok[:, tc * 10 + e:tc * 10 + e + 1], identf,
                             [bcombtok], pub=(i == 3))
                k.copy(combrow[0:1, q4 * 512:(q4 + 1) * 512],
                       ps[4][0:1, 0:512], reads=[PSB[4]], writes=[bcombrow])
                release(4)
            for tq in range(4):
                k.mm(PSB[5], ps[5][:, 0:512], onesb_r,
                     combrow[0:1, tq * 512:(tq + 1) * 512], [bcombrow],
                     last=True)
                k.act(AF.Copy, combbc[ep][:, tq * 512:(tq + 1) * 512],
                      ps[5][:, 0:512], reads=[PSB[5]], writes=[bcombbc[ep]])
                release(5)
        for hc in range(NHC):
            wp, bwp = w1p[hc % 3], bw1p[hc % 3]
            k.dma(wp, D["w1"][e, hc].rearrange("p a b -> p (a b)"),
                  writes=[bwp], eng="pool")
            for tq in range(4):
                pb = (hc * 4 + tq) % 2
                for fc in range(NFC):
                    k.mm(PSB[pb], ps[pb][:, 0:512],
                         wp[:, fc * 128:(fc + 1) * 128],
                         xn_v[:, fc, 3 + tq * 512:3 + tq * 512 + 512],
                         [bwp, bxn], last=(fc == 7))
                hp = tq % 2
                k.act(AF.Gelu, hraw[hp], ps[pb][:, 0:512], reads=[PSB[pb]],
                      writes=[bhraw[hp]],
                      bias=b1c[:, e * 16 + hc:e * 16 + hc + 1])
                release(pb)
                if e < 8:
                    k.tt(hsb_v[:, hc, tq * 512:(tq + 1) * 512], hraw[hp],
                         combbc[ep][:, tq * 512:(tq + 1) * 512], ALU.mult,
                         reads=[bhraw[hp], bcombbc[ep]], writes=[bhsb],
                         eng="dve")
                else:
                    k.ts1(hsb_v[:, hc, tq * 512:(tq + 1) * 512], hraw[hp],
                          0.5, ALU.mult, reads=[bhraw[hp]], writes=[bhsb],
                          eng="dve")
        for dc in range(NDC):
            wp2, bp2 = w2p[dc % 2], bw2p[dc % 2]
            k.dma(wp2, D["w2"][e, dc].rearrange("p a b -> p (a b)"),
                  writes=[bp2], eng="pool")
            for tq in range(4):
                pb = 2 + (dc * 4 + tq) % 2
                for hc in range(NHC):
                    k.mm(PSB[pb], ps[pb][:, 0:512],
                         wp2[:, hc * 128:(hc + 1) * 128],
                         hsb_v[:, hc, tq * 512:(tq + 1) * 512], [bp2, bhsb],
                         last=(hc == 15))
                k.tt(acc_v[:, dc, tq * 512:(tq + 1) * 512],
                     acc_v[:, dc, tq * 512:(tq + 1) * 512], ps[pb][:, 0:512],
                     ALU.add, reads=[PSB[pb]], writes=[bacc])
                release(pb)
    k.dma(out_d.rearrange("(a p) n -> p a n", p=128), acc_v, reads=[bacc],
          writes=[bout])
    k.S.wait("sp", bout.wtok)

    k.S.emit(nc, sems)
    return nc, list(dbg.keys())


_CACHE = {}


def kernel(**inputs):
    u = np.asarray(inputs["L0_attn_u"], np.float32)
    shared = _prep_shared(inputs)
    if "nc" not in _CACHE:
        _CACHE["nc"] = build_program()
    nc, dbg_names = _CACHE["nc"]
    in_maps = []
    for c in range(8):
        m = dict(shared)
        m.update(_prep_core(u, c))
        in_maps.append(m)
    res = run_bass_kernel_spmd(nc, in_maps, core_ids=list(range(8)))
    out = np.empty((4, 4096, 1024), np.float32)
    for c in range(8):
        b, half = divmod(c, 2)
        out[b, half * TPC:(half + 1) * TPC, :] = res.results[c]["out"].T
    if _DEBUG:
        kernel.dbg = res.results
    return out


# version 09
# speedup vs baseline: 1.2083x; 1.0089x over previous
import numpy as np
import concourse.bass as bass
import concourse.mybir as mybir
from concourse.bass_utils import run_bass_kernel_spmd
from concourse.alu_op_type import AluOpType as ALU

F32 = mybir.dt.float32
BF16 = mybir.dt.bfloat16
AF = mybir.ActivationFunctionType

DM, DI, DS, NH, HD = 1024, 2048, 64, 16, 128
NFC = 8
NCC = 17
TPC = 2048
GQ = 512
NG = 4
Q = 128
NCH = 4
NZC, NDC, NKC, NHC = 16, 8, 16, 16
SW1 = 2192

_DEBUG = 0
_DBG_G = 0
ARENA_COLS = 53200



def _prep_shared(inp):
    f = lambda x: np.ascontiguousarray(np.asarray(x, np.float32))
    d = {}
    WipT = f(inp["L0_attn_in_proj_w"]).T
    d["wsw"] = np.ascontiguousarray(
        np.concatenate([WipT[:, 2048:4096], WipT[:, 4096:4160],
                        WipT[:, 4160:4224], WipT[:, 4224:4240]], axis=1))
    d["wz"] = np.ascontiguousarray(
        WipT[:, 0:2048].reshape(8, 128, 16, 128).transpose(2, 1, 0, 3))
    d["wo"] = np.ascontiguousarray(
        f(inp["L0_attn_out_w"]).T.reshape(16, 128, 8, 128).transpose(2, 1, 0, 3))
    d["convw"] = np.ascontiguousarray(
        f(inp["L0_attn_conv_w"]).reshape(17, 128, 4).transpose(1, 0, 2)).reshape(128, 68)
    d["convb"] = np.ascontiguousarray(f(inp["L0_attn_conv_b"]).reshape(17, 128).T)
    d["dtb"] = f(inp["L0_attn_dt_bias"]).reshape(16, 1)
    d["alog16"] = f(inp["L0_attn_A_log"]).reshape(16, 1)
    d["alogr"] = f(inp["L0_attn_A_log"]).reshape(1, 16)
    d["drow"] = f(inp["L0_attn_D"]).reshape(1, 16)
    d["rmsw"] = np.ascontiguousarray(f(inp["L0_attn_rms_w"]).reshape(16, 128).T)
    d["rw"] = np.ascontiguousarray(
        f(inp["L0_ffn_router_w"]).reshape(8, 128, 8).transpose(1, 0, 2)).reshape(128, 64)
    d["rb"] = f(inp["L0_ffn_router_b"]).reshape(8, 1)
    w1c = np.concatenate([f(inp["L0_ffn_w1"]), f(inp["L0_ffn_sw1"])], axis=0)
    d["w1"] = np.ascontiguousarray(
        w1c.reshape(10, 8, 128, 16, 128).transpose(0, 3, 2, 1, 4))
    w2c = np.concatenate([f(inp["L0_ffn_w2"]), f(inp["L0_ffn_sw2"])], axis=0)
    d["w2"] = np.ascontiguousarray(
        w2c.reshape(10, 16, 128, 8, 128).transpose(0, 3, 2, 1, 4))
    b1c = np.concatenate([f(inp["L0_ffn_b1"]), f(inp["L0_ffn_sb1"])], axis=0)
    d["b1"] = np.ascontiguousarray(
        b1c.reshape(10, 16, 128).transpose(2, 0, 1)).reshape(128, 160)
    b2c = np.concatenate([f(inp["L0_ffn_b2"]), f(inp["L0_ffn_sb2"])], axis=0)
    d["b2"] = np.ascontiguousarray(
        np.concatenate([b2c, np.zeros((6, DM), np.float32)], axis=0))
    d["identf"] = np.eye(128, dtype=np.float32)
    triu = np.triu(np.ones((128, 128), np.float32))
    d["triu"] = triu
    mn = np.where(triu > 0, 0.0, -1e30).astype(np.float32)
    d["maskneg"] = np.tile(mn, (1, 4))
    d["onesf"] = np.ones((128, 128), np.float32)
    d["onescol"] = np.ones((128, 1), np.float32)
    d["cvals"] = np.tile(np.array([[1e-5, 1.0, -0.5, -1.0, 1.0 / 1024,
                                    1.0 / 2048, 0.5, 0.0]], np.float32),
                         (128, 1))
    d["negonesf"] = -np.ones((128, 128), np.float32)
    return d


def _prep_core(u, core):
    b, half = core // 2, core % 2
    z3 = np.zeros((3, DM), np.float32)
    if half == 0:
        uo = np.concatenate([z3, u[b, 0:TPC]], axis=0)
        up, mask = uo, 0.0
    else:
        uo = u[b, TPC - 3:2 * TPC]
        up = np.concatenate([z3, u[b, 0:TPC]], axis=0)
        mask = 1.0
    return {"u_own": np.ascontiguousarray(uo.T),
            "u_prev": np.ascontiguousarray(up.T),
            "smask": np.full((64, 1), mask, np.float32)}



class Sched:
    ENG = ("pe", "act", "dve", "pool", "sp")

    def __init__(self):
        self.items = {e: [] for e in self.ENG}
        self.cnt = dict.fromkeys(list(self.ENG) + ["dsp", "dpool"], 0)
        self.hwm = {e: {} for e in self.ENG}

    def op(self, eng, fn, deps=(), pub=False, dma=False):
        for tok in deps:
            if tok is None:
                continue
            s, v = tok
            if s == eng:
                continue
            if v > self.hwm[eng].get(s, 0):
                self.items[eng].append(("wait", s, v))
                self.hwm[eng][s] = v
        if dma:
            sem, amt = ("dsp" if eng == "sp" else "dpool"), 16
        elif pub:
            sem, amt = eng, 1
        else:
            sem = None
        if sem is not None:
            self.cnt[sem] += amt
            tok = (sem, self.cnt[sem])
        else:
            tok = (eng, self.cnt[eng])
        self.items[eng].append(("op", fn, sem, amt if sem else 0))
        return tok

    def wait(self, eng, tok):
        if tok is None:
            return
        s, v = tok
        if s == eng:
            return
        if v > self.hwm[eng].get(s, 0):
            self.items[eng].append(("wait", s, v))
            self.hwm[eng][s] = v

    def emit(self, nc, sems):
        S = self

        def mk(name):
            def body(h):
                for it in S.items[name]:
                    if it[0] == "wait":
                        h.wait_ge(sems[it[1]], it[2])
                    else:
                        inst = it[1](h)
                        if it[2] is not None:
                            inst.then_inc(sems[it[2]], it[3])
            return body
        with nc.Block() as block:
            block.tensor(mk("pe"))
            block.scalar(mk("act"))
            block.vector(mk("dve"))
            block.gpsimd(mk("pool"))
            block.sync(mk("sp"))


class Buf:
    __slots__ = ("wtok", "rtoks", "fresh", "widx", "nw")

    def __init__(self):
        self.wtok = None
        self.rtoks = []
        self.fresh = True
        self.widx = None
        self.nw = 9999


class Arena:
    def __init__(self, ap):
        self.ap = ap
        self.off = 0
        self.marks = {}

    def f32(self, cols, parts=128):
        c0 = self.off
        self.off += cols
        assert self.off <= ARENA_COLS, ("arena overflow", self.off)
        return self.ap[0:parts, c0:c0 + cols]

    def bf16(self, cols, parts=128):
        cf = (cols + 1) // 2
        c0 = self.off
        self.off += cf
        assert self.off <= ARENA_COLS, ("arena overflow", self.off)
        return self.ap[0:parts, c0:c0 + cf].bitcast(BF16)[0:parts, 0:cols]

    def mark(self, n):
        self.marks[n] = self.off

    def reset(self, n):
        self.off = self.marks[n]


class K:
    GUARD_DIST = 6
    GUARD_W = 64

    def __init__(self):
        self.S = Sched()
        self.seq = dict.fromkeys(Sched.ENG, 0)
        self.ndrain = 0

    def do(self, eng, fn, reads=(), writes=(), pub=True, dma=False, ow=9999):
        if eng in ("dve", "act", "pool") and not dma:
            for b in list(reads) + list(writes):
                if (b.widx is not None and b.widx[0] == eng
                        and b.nw < self.GUARD_W
                        and self.seq[eng] - b.widx[1] <= self.GUARD_DIST):
                    self.S.op(eng, lambda e: e.drain(), ())
                    self.seq[eng] += 1
                    self.ndrain += 1
                    break
        deps = []
        for b in reads:
            deps.append(b.wtok)
        for b in writes:
            deps.append(b.wtok)
            deps.extend(b.rtoks)
        tok = self.S.op(eng, fn, deps, pub=pub, dma=dma)
        self.seq[eng] += 1
        for b in writes:
            b.wtok = tok
            b.rtoks = []
            b.widx = (eng, self.seq[eng])
            b.nw = ow
        for b in reads:
            b.rtoks.append(tok)
        return tok

    def mm(self, bank, out, lhsT, rhs, reads, last, stop=None, pub=None):
        st = bank.fresh
        bank.fresh = False
        sp = last if stop is None else stop
        pb = last if pub is None else pub
        return self.do("pe",
                       lambda e, o=out, l=lhsT, r=rhs, a=st, b=sp:
                       e.matmul(o, l, r, start=a, stop=b),
                       reads=reads, writes=[bank], pub=pb)

    def transp(self, bank, out, in_, ident, reads, pub=True):
        bank.fresh = False
        return self.do("pe",
                       lambda e, o=out, i=in_, d=ident: e.transpose(o, i, d),
                       reads=reads, writes=[bank], pub=pub)

    def act(self, fn, out, in_, reads, writes, bias=0.0, scale=1.0, pub=True):
        return self.do("act",
                       lambda e, o=out, i=in_, f=fn, b=bias, s=scale:
                       e.activation(o, i, f, bias=b, scale=s),
                       reads=reads, writes=writes, pub=pub,
                       ow=int(np.prod(out.shape[1:])))

    def copy(self, out, in_, reads, writes, eng="dve", pub=True):
        if eng == "act":
            fn = lambda e, o=out, i=in_: e.copy(o, i)
        else:
            fn = lambda e, o=out, i=in_: e.tensor_copy(o, i)
        return self.do(eng, fn, reads=reads, writes=writes, pub=pub,
                       ow=int(np.prod(out.shape[1:])))

    def tt(self, out, i0, i1, op, reads, writes, eng="dve", pub=True):
        return self.do(eng, lambda e, o=out, a=i0, b=i1, p=op:
                       e.tensor_tensor(o, a, b, p),
                       reads=reads, writes=writes, pub=pub,
                       ow=int(np.prod(out.shape[1:])))

    def ts1(self, out, i0, s1, op0, reads, writes, eng="dve", pub=True):
        return self.do(eng, lambda e, o=out, a=i0, x=s1, p0=op0:
                       e.tensor_scalar(o, a, x, None, p0),
                       reads=reads, writes=writes, pub=pub,
                       ow=int(np.prod(out.shape[1:])))

    def ts2(self, out, i0, s1, s2, op0, op1, reads, writes, eng="dve", pub=True):
        return self.do(eng, lambda e, o=out, a=i0, x=s1, y=s2, p0=op0, p1=op1:
                       e.tensor_scalar(o, a, x, y, p0, p1),
                       reads=reads, writes=writes, pub=pub,
                       ow=int(np.prod(out.shape[1:])))

    def stt(self, out, i0, sc, i1, op0, op1, reads, writes, eng="dve", pub=True):
        return self.do(eng, lambda e, o=out, a=i0, x=sc, b=i1, p0=op0, p1=op1:
                       e.scalar_tensor_tensor(o, a, x, b, p0, p1),
                       reads=reads, writes=writes, pub=pub,
                       ow=int(np.prod(out.shape[1:])))

    def red(self, out, in_, op, reads, writes, eng="dve", pub=True):
        return self.do(eng, lambda e, o=out, i=in_, p=op:
                       e.tensor_reduce(o, i, mybir.AxisListType.X, p),
                       reads=reads, writes=writes, pub=pub,
                       ow=int(np.prod(out.shape[1:])))

    def dma(self, out, in_, reads=(), writes=(), eng="sp", accum=False):
        if accum:
            fn = lambda e, o=out, i=in_: e.dma_start(out=o, in_=i,
                                                     accum_op=ALU.add)
        else:
            fn = lambda e, o=out, i=in_: e.dma_start(out=o, in_=i)
        return self.do(eng, fn, reads=reads, writes=writes, dma=True)

    def memset(self, ap_, val, reads=(), writes=(), eng="pool", pub=True):
        return self.do(eng, lambda e, a=ap_, v=val: e.memset(a, v),
                       reads=reads, writes=writes, pub=pub)



def build_program():
    nc = bass.Bass("TRN2", target_bir_lowering=False)
    k = K()

    def din(name, shape):
        return nc.dram_tensor(name, list(shape), F32, kind="ExternalInput").ap()

    D = {n: din(n, s) for n, s in [
        ("u_own", (DM, 2051)), ("u_prev", (DM, 2051)), ("smask", (64, 1)),
        ("wsw", (DM, SW1)), ("wz", (16, 128, 8, 128)), ("wo", (8, 128, 16, 128)),
        ("convw", (128, 68)), ("convb", (128, 17)), ("dtb", (16, 1)),
        ("alog16", (16, 1)), ("alogr", (1, 16)), ("drow", (1, 16)),
        ("rmsw", (128, 16)), ("rw", (128, 64)), ("rb", (8, 1)),
        ("w1", (10, 16, 128, 8, 128)), ("w2", (10, 8, 128, 16, 128)),
        ("b1", (128, 160)), ("b2", (16, DM)), ("identf", (128, 128)),
        ("triu", (128, 128)), ("maskneg", (128, 512)), ("onesf", (128, 128)),
        ("negonesf", (128, 128)), ("cvals", (128, 8)),
        ("onescol", (128, 1))]}
    scratch = nc.dram_tensor("scr", [DM, TPC], F32, kind="Internal").ap()
    out_d = nc.dram_tensor("out", [DM, TPC], F32, kind="ExternalOutput").ap()
    dbg = {}
    if _DEBUG:
        def dout(name, shape, dt):
            dbg[name] = nc.dram_tensor(name, list(shape), dt,
                                       kind="ExternalOutput").ap()
        dout("dbg_xn", (128, NFC * 2051), BF16)
        dout("dbg_gates", (8, TPC), F32)
        dout("dbg_xbc", (128, NCC * GQ), BF16)
        dout("dbg_dt", (16, GQ), F32)
        dout("dbg_y", (128, NZC * GQ), BF16)
        dout("dbg_h", (64, NH * HD), F32)
        dout("dbg_hp", (64, NH * HD), F32)
        dout("dbg_scr", (DM, TPC), F32)
        dout("dbg_ws", (128, 16), F32)
        dout("dbg_dec", (64, 16), F32)
        dout("dbg_cta", (128, 16), F32)
        dout("dbg_cq", (1, 16), F32)
        dout("dbg_h0", (64, NH * HD), F32)
        dout("dbg_dtvp", (16, GQ), F32)
        dout("dbg_xbcp", (128, NCC * GQ), BF16)
        dout("dbg_bt", (128, DS), BF16)
        dout("dbg_xt", (128, NH * Q), BF16)
        dout("dbg_ps5", (128, 16), F32)
        dout("dbg_rows", (1, 1545), F32)
        dout("dbg_xn2", (128, NFC * 2051), BF16)

    ctx = []

    def enter(cm):
        v = cm.__enter__()
        ctx.append(cm)
        return v

    arena = enter(nc.sbuf_tensor("arena", [128, ARENA_COLS], F32))[:, :]
    A = Arena(arena)
    ps = [enter(nc.psum_tensor("ps%d" % i, [128, 512], F32))[:, :]
          for i in range(8)]
    sems = {n: enter(nc.semaphore("sem_" + n))
            for n in ("pe", "act", "dve", "pool", "dsp", "dpool")}

    PSB = [Buf() for _ in range(8)]

    def release(i):
        PSB[i].fresh = True

    identf = A.f32(128)
    identb = A.bf16(128)
    triu_b = A.bf16(128)
    maskneg4 = A.f32(512)
    onesb = A.bf16(128)
    negonesb = A.bf16(128)
    onesf_c = A.f32(1)
    onesb_c = A.bf16(2)[:, 0:1]
    onesf_r = A.f32(128, parts=1)
    onesb_r = A.bf16(128, parts=1)
    cvals_s = A.f32(8)
    xn = A.bf16(NFC * 2051)
    gates = A.f32(TPC, parts=8)
    A.mark("perm")

    convw = A.f32(68)
    convb = A.f32(17)
    dtb = A.f32(1, parts=16)
    alog16 = A.f32(1, parts=16)
    alogr = A.f32(16, parts=1)
    abc128 = A.f32(16)
    dbc128 = A.f32(16)
    drow_s = A.f32(16, parts=1)
    rmsw = A.f32(16)
    rw = A.f32(64)
    rb = A.f32(1, parts=8)
    csrw = A.f32(1, parts=8)
    smask_s = A.f32(1, parts=64)
    H = A.f32(NH * HD, parts=64)
    Hbf = A.bf16(NH * HD, parts=64)
    sumsq = A.f32(GQ, parts=1)
    erow = A.bf16(NH * Q, parts=1)
    cqrow = A.f32(16, parts=1)
    dtsum = A.f32(1, parts=16)
    ug = A.f32(NFC * 515)
    t1n = A.f32(515)
    sqln = A.bf16(515)
    rows = A.f32(1545, parts=1)
    wsw = A.bf16(NFC * SW1)
    preconv = [A.bf16(515)]
    convacc = A.bf16(512)
    xbc = A.bf16(NCC * GQ)
    dtg = A.f32(515, parts=16)
    dtv = A.f32(GQ, parts=16)
    dtT = [A.f32(16), A.f32(16)]
    dta = [A.bf16(NH * Q), A.bf16(NH * Q)]
    ctA = [A.f32(16), A.f32(16)]
    wS = [A.f32(16), A.f32(16)]
    decay = [A.f32(16, parts=64), A.f32(16, parts=64)]
    bT = [A.bf16(DS), A.bf16(DS)]
    bw4 = [A.bf16(4 * DS), A.bf16(4 * DS)]
    xT = [A.bf16(NH * Q)]
    lmask4 = [A.f32(512), A.f32(512)]
    expl4 = lmask4
    stl4 = [A.bf16(512), A.bf16(512)]
    cw4 = [A.bf16(512, parts=64), A.bf16(512, parts=64)]
    yg = A.bf16(NZC * GQ)
    sqs = A.bf16(4 * Q)
    ccpy = A.bf16(Q, parts=64)
    zsig = [A.bf16(GQ), A.bf16(GQ)]
    wzp = [A.bf16(NFC * 128), A.bf16(NFC * 128)]
    rstd_bc = A.f32(GQ)
    ostage = [A.f32(GQ), A.f32(GQ)]
    A.mark("ap_shared")
    wop = [A.bf16(NKC * 128), A.bf16(NKC * 128)]
    A.mark("ap_end")

    A.reset("perm")
    acc = A.f32(NDC * TPC)
    hsb = A.bf16(NHC * TPC)
    w1p = [A.bf16(NFC * 128) for _ in range(3)]
    w2p = [A.bf16(NKC * 128) for _ in range(2)]
    hraw = [A.bf16(GQ), A.bf16(GQ)]
    combtok = A.f32(160)
    comb10 = A.bf16(TPC, parts=16)
    combrow = A.bf16(TPC, parts=1)
    combbc = [A.bf16(TPC), A.bf16(TPC)]
    b1c = A.f32(160)
    b2c = A.bf16(DM, parts=16)
    gtt = A.f32(8)
    gsm = A.f32(8)
    is1t = A.f32(8)
    g2t = A.f32(8)
    is2t = A.f32(8)
    tmp8 = A.f32(8)
    f_high = A.off
    assert max(A.marks["ap_end"], f_high) <= ARENA_COLS

    xn_v = xn.rearrange("p (a b) -> p a b", a=NFC)
    ug_v = ug.rearrange("p (a b) -> p a b", a=NFC)
    wsw_v = wsw.rearrange("p (a b) -> p a b", a=NFC)
    rw_v = rw.rearrange("p (a b) -> p a b", a=NFC)
    xbc_v = xbc.rearrange("p (a b) -> p a b", a=NCC)
    yg_v = yg.rearrange("p (a b) -> p a b", a=NZC)
    acc_v = acc.rearrange("p (a b) -> p a b", a=NDC)
    hsb_v = hsb.rearrange("p (a b) -> p a b", a=NHC)

    NB = lambda: Buf()
    bconst = NB()
    bug, bxn, bgates, bgtmp = NB(), NB(), NB(), NB()
    brows, bsqln, bt1n, brst = NB(), NB(), NB(), NB()
    bwsw = NB()
    bpre = [NB()]
    bconvacc = NB()
    bxbc, bdtg, bdtv = NB(), NB(), NB()
    bdtT = [NB(), NB()]
    bdta = [NB(), NB()]
    bctA = [NB(), NB()]
    bwS = [NB(), NB()]
    bdecay = [NB(), NB()]
    bbT = [NB(), NB()]
    bbw = [NB(), NB()]
    bxT = [NB()]
    blm = [NB(), NB()]
    bexpl = [NB(), NB()]
    bstl = [NB(), NB()]
    bcw = [NB(), NB()]
    byg, bsqs = NB(), NB()
    bccpy = NB()
    bzsig = [NB(), NB()]
    bwzp = [NB(), NB()]
    bwop = [NB(), NB()]
    brstdbc = NB()
    bost = [NB(), NB()]
    bH, bHbf = NB(), NB()
    bsumsq, berow, bcq, bdtsum = NB(), NB(), NB(), NB()
    bscr, bout = NB(), NB()
    bacc, bhsb = NB(), NB()
    bw1p = [NB(), NB(), NB()]
    bw2p = [NB(), NB()]
    bhraw = [NB(), NB()]
    bcombtok, bcomb10, bcombrow = NB(), NB(), NB()
    bcombbc = [NB(), NB()]
    bgtt, bgsm = NB(), NB()
    bis1, bg2, bis2, btmp8 = NB(), NB(), NB(), NB()

    for dst, src in [(identf, "identf"), (maskneg4, "maskneg"), (convw, "convw"),
                     (convb, "convb"), (dtb, "dtb"), (alog16, "alog16"),
                     (alogr, "alogr"), (rmsw, "rmsw"), (rw, "rw"), (rb, "rb"),
                     (drow_s, "drow"), (smask_s, "smask")]:
        k.dma(dst, D[src], writes=[bconst])
    for dst, src in [(identb, "identf"), (triu_b, "triu"), (onesb, "onesf"),
                     (negonesb, "negonesf")]:
        k.dma(dst, D[src], writes=[bconst], eng="pool")
    k.dma(onesf_c, D["onescol"], writes=[bconst], eng="pool")
    k.dma(onesb_c, D["onescol"], writes=[bconst], eng="pool")
    k.dma(onesf_r, D["onesf"][0:1, :], writes=[bconst], eng="pool")
    k.dma(onesb_r, D["onesf"][0:1, :], writes=[bconst], eng="pool")
    for fc in range(NFC):
        k.dma(wsw_v[:, fc, :], D["wsw"][fc * 128:(fc + 1) * 128, :],
              writes=[bwsw], eng="pool")
    k.memset(H[0:64, :], 0.0, writes=[bH])
    tcv = k.dma(cvals_s, D["cvals"], writes=[bconst])
    for i, v in enumerate([1e-5, 1.0, -0.5, -1.0, 1.0 / 1024, 1.0 / 2048,
                           0.5, 0.0]):
        nc.const_aps.register(F32, v, cvals_s[:, i:i + 1])
    for e in ("pe", "act", "dve", "pool"):
        k.S.wait(e, tcv)

    k.act(AF.Exp, alog16, alog16, reads=[bconst], writes=[bconst])
    k.ts1(alog16, alog16, -1.0, ALU.mult, reads=[], writes=[bconst])
    k.act(AF.Exp, alogr, alogr, reads=[bconst], writes=[bconst])
    k.ts1(alogr, alogr, -1.0, ALU.mult, reads=[], writes=[bconst])
    k.mm(PSB[6], ps[6][:, 0:16], onesf_r, alogr, [bconst], last=True)
    k.copy(abc128, ps[6][:, 0:16], reads=[PSB[6]], writes=[bconst])
    release(6)
    for fc in range(NFC):
        k.mm(PSB[6], ps[6][0:8, 16:17], rw_v[:, fc, :], onesf_c, [bconst],
             last=(fc == 7))
    k.copy(csrw[0:8, 0:1], ps[6][0:8, 16:17], reads=[PSB[6]], writes=[bconst])
    release(6)
    k.mm(PSB[7], ps[7][:, 0:16], onesf_r, drow_s, [bconst], last=True)
    k.copy(dbc128, ps[7][:, 0:16], reads=[PSB[7]], writes=[bconst])
    release(7)

    def front_stats(u_dram, g, xdst, bxd, xw0, do_gates):
        w0 = g * GQ
        uin = u_dram.rearrange("(a p) n -> p a n", p=128)[:, :, w0:w0 + 515]
        k.dma(ug_v, uin, writes=[bug])
        for fc in range(NFC):
            k.act(AF.Square, sqln, ug_v[:, fc, :], reads=[bug], writes=[bsqln])
            k.mm(PSB[4], ps[4][0:1, 0:512], onesf_c, ug_v[:, fc, 0:512],
                 [bug, bconst], last=(fc == 7), pub=True)
            k.mm(PSB[5], ps[5][0:1, 0:3], onesf_c, ug_v[:, fc, 512:515],
                 [bug, bconst], last=False, stop=(fc == 7), pub=True)
            k.mm(PSB[6], ps[6][0:1, 0:512], onesb_c, sqln[:, 0:512],
                 [bsqln, bconst], last=(fc == 7), pub=True)
            k.mm(PSB[5], ps[5][0:1, 4:7], onesb_c, sqln[:, 512:515],
                 [bsqln, bconst], last=(fc == 7), pub=True)
        mean_r = rows[0:1, 0:515]
        msq_r = rows[0:1, 515:1030]
        rstd_r = rows[0:1, 1030:1545]
        k.act(AF.Copy, mean_r[0:1, 0:512], ps[4][0:1, 0:512], reads=[PSB[4]],
              writes=[brows], scale=1.0 / DM)
        k.act(AF.Copy, mean_r[0:1, 512:515], ps[5][0:1, 0:3], reads=[PSB[5]],
              writes=[brows], scale=1.0 / DM)
        k.act(AF.Copy, msq_r[0:1, 0:512], ps[6][0:1, 0:512], reads=[PSB[6]],
              writes=[brows], scale=1.0 / DM)
        k.act(AF.Copy, msq_r[0:1, 512:515], ps[5][0:1, 4:7], reads=[PSB[5]],
              writes=[brows], scale=1.0 / DM)
        release(4), release(5), release(6)
        k.tt(rstd_r, mean_r, mean_r, ALU.mult, reads=[brows], writes=[brows])
        k.tt(rstd_r, msq_r, rstd_r, ALU.subtract, reads=[], writes=[brows])
        k.act(AF.Ln, rstd_r, rstd_r, reads=[brows], writes=[brows], bias=1e-5)
        k.act(AF.Exp, rstd_r, rstd_r, reads=[], writes=[brows], scale=-0.5)
        k.tt(mean_r, mean_r, rstd_r, ALU.mult, reads=[brows], writes=[brows])
        mrs_r = mean_r
        k.mm(PSB[4], ps[4][:, 0:512], onesf_r, rstd_r[0:1, 0:512], [brows],
             last=True)
        k.mm(PSB[5], ps[5][:, 8:11], onesf_r, rstd_r[0:1, 512:515], [brows],
             last=False, stop=True)
        k.mm(PSB[6], ps[6][:, 0:512], onesf_r, mrs_r[0:1, 0:512], [brows],
             last=True)
        k.mm(PSB[5], ps[5][:, 12:15], onesf_r, mrs_r[0:1, 512:515], [brows],
             last=True)
        if _DEBUG and do_gates and g == 3:
            k.copy(ostage[1][:, 0:16], ps[5][:, 0:16], reads=[PSB[5]],
                   writes=[bost[1]])
            k.dma(dbg["dbg_ps5"], ostage[1][:, 0:16], reads=[bost[1]])
            k.dma(dbg["dbg_rows"], rows[0:1, :], reads=[brows])
        for fc in range(NFC):
            k.tt(t1n[:, 0:512], ug_v[:, fc, 0:512], ps[4][:, 0:512], ALU.mult,
                 reads=[bug, PSB[4]], writes=[bt1n])
            k.tt(xdst[:, fc, xw0:xw0 + 512], t1n[:, 0:512], ps[6][:, 0:512],
                 ALU.subtract, reads=[bt1n, PSB[6]], writes=[bxd])
            k.tt(t1n[:, 512:515], ug_v[:, fc, 512:515], ps[5][:, 8:11],
                 ALU.mult, reads=[bug, PSB[5]], writes=[bt1n])
            k.tt(xdst[:, fc, xw0 + 512:xw0 + 515], t1n[:, 512:515],
                 ps[5][:, 12:15], ALU.subtract, reads=[bt1n, PSB[5]],
                 writes=[bxd])
        if _DEBUG and do_gates and g == 3:
            k.dma(dbg["dbg_xn2"], xn, reads=[bxn])
        if do_gates:
            for fc in range(NFC):
                k.mm(PSB[7], ps[7][0:8, 0:512], rw_v[:, fc, :],
                     ug_v[:, fc, 3:515], [bug, bconst], last=(fc == 7))
            g0 = gates[0:8, w0:w0 + 512]
            k.copy(g0, ps[7][0:8, 0:512], reads=[PSB[7]], writes=[bgates])
            release(7)
            k.mm(PSB[7], ps[7][0:8, 0:512], onesf_r[0:1, 0:8],
                 rstd_r[0:1, 3:515], [brows], last=True)
            k.tt(g0, g0, ps[7][0:8, 0:512], ALU.mult, reads=[PSB[7]],
                 writes=[bgates])
            release(7)
            k.mm(PSB[7], ps[7][0:8, 0:512], onesf_r[0:1, 0:8],
                 mrs_r[0:1, 3:515], [brows], last=True)
            gtmp = ostage[0][0:8, :]
            k.ts1(gtmp, ps[7][0:8, 0:512], csrw[0:8, 0:1], ALU.mult,
                  reads=[PSB[7], bconst], writes=[bost[0]])
            release(7)
            k.tt(g0, g0, gtmp, ALU.subtract, reads=[bost[0]],
                 writes=[bgates])
            k.ts1(g0, g0, rb[0:8, 0:1], ALU.add, reads=[bconst],
                  writes=[bgates])
        release(4), release(5), release(6)

    def front_mm(xdst, bxd, xw0, do_y=False):
        for cc in range(NCC + 1):
            M = 128 if cc < NCC else 16
            pa, pb = ps[cc % 2], ps[2 + cc % 2]
            ba, bb = PSB[cc % 2], PSB[2 + cc % 2]
            for fc in range(NFC):
                lhs = wsw_v[:, fc, cc * 128:cc * 128 + M]
                k.mm(ba, pa[0:M, 0:512], lhs, xdst[:, fc, xw0:xw0 + 512],
                     [bxd, bwsw], last=(fc == 7))
                k.mm(bb, pb[0:M, 0:3], lhs, xdst[:, fc, xw0 + 512:xw0 + 515],
                     [bxd, bwsw], last=(fc == 7))
            if cc < NCC:
                pc, bpc = preconv[0], bpre[0]
                k.act(AF.Copy, pc[:, 0:512], pa[0:128, 0:512], reads=[ba],
                      writes=[bpc])
                k.act(AF.Copy, pc[:, 512:515], pb[0:128, 0:3], reads=[bb],
                      writes=[bpc])
                release(cc % 2), release(2 + cc % 2)
                ceng = "dve"
                k.ts1(convacc, pc[:, 0:512], convw[:, 4 * cc:4 * cc + 1],
                      ALU.mult, reads=[bpc, bconst], writes=[bconvacc],
                      eng=ceng)
                for t in range(1, 4):
                    k.stt(convacc, pc[:, t:512 + t],
                          convw[:, 4 * cc + t:4 * cc + t + 1], convacc,
                          ALU.mult, ALU.add, reads=[bpc, bconst],
                          writes=[bconvacc], eng=ceng)
                k.act(AF.Silu, xbc_v[:, cc, :], convacc, reads=[bconvacc],
                      writes=[bxbc], bias=convb[:, cc:cc + 1])
                if cc < 16 and do_y:
                    k.ts1(yg_v[:, cc, :], xbc_v[:, cc, :],
                          dbc128[:, cc:cc + 1], ALU.mult,
                          reads=[bxbc, bconst], writes=[byg], eng="dve")
            else:
                k.act(AF.Copy, dtg[0:16, 0:512], pa[0:16, 0:512], reads=[ba],
                      writes=[bdtg])
                k.act(AF.Copy, dtg[0:16, 512:515], pb[0:16, 0:3], reads=[bb],
                      writes=[bdtg])
                release(cc % 2), release(2 + cc % 2)
                k.ts2(dtv[0:16, :], dtg[0:16, 3:515], dtb[0:16, 0:1], 0.0,
                      ALU.add, ALU.max, reads=[bdtg, bconst], writes=[bdtv])
                k.act(AF.Abs, dtg[0:16, 3:515], dtg[0:16, 3:515], reads=[bdtg],
                      writes=[bdtg], bias=dtb[0:16, 0:1])
                k.act(AF.Exp, dtg[0:16, 3:515], dtg[0:16, 3:515], reads=[],
                      writes=[bdtg], scale=-1.0)
                k.act(AF.Ln, dtg[0:16, 3:515], dtg[0:16, 3:515], reads=[],
                      writes=[bdtg], bias=1.0)
                k.tt(dtv[0:16, :], dtv[0:16, :], dtg[0:16, 3:515], ALU.add,
                     reads=[bdtg], writes=[bdtv])

    def chunk(gc, c, do_y):
        par = gc % 2
        c0 = c * Q
        dtc = dtv[0:16, c0:c0 + Q]
        k.transp(PSB[4], ps[4][:, 0:16], dtc, identf[0:16, 0:16],
                 [bdtv, bconst], pub=True)
        k.copy(dtT[par], ps[4][:, 0:16], reads=[PSB[4]], writes=[bdtT[par]], eng="act")
        for h in range(NH):
            k.ts2(dta[par][:, h * Q:(h + 1) * Q], triu_b, dtT[par][:, h:h + 1],
                  abc128[:, h:h + 1], ALU.mult, ALU.mult,
                  reads=[bconst, bdtT[par]], writes=[bdta[par]],
                  pub=(h % 4 == 3))
        for h in range(NH):
            k.mm(PSB[4], ps[4][0:1, 64 + h:65 + h], onesb_c,
                 dta[par][:, h * Q + Q - 1:h * Q + Q], [bdta[par], bconst],
                 last=(h == 15), pub=True)
        k.copy(cqrow, ps[4][0:1, 64:80], reads=[PSB[4]], writes=[bcq], eng="act")
        for h in range(NH):
            k.mm(PSB[4], ps[4][:, 16 + h:17 + h],
                 dta[par][:, h * Q:(h + 1) * Q], onesb_c, [bdta[par], bconst],
                 last=(h == 15))
        k.copy(ctA[par], ps[4][:, 16:32], reads=[PSB[4]], writes=[bctA[par]], eng="act")
        k.mm(PSB[4], ps[4][:, 32:48], onesf_r, cqrow, [bcq], last=True)
        k.stt(wS[par], ctA[par], -1.0, ps[4][:, 32:48], ALU.mult, ALU.add,
              reads=[bctA[par], PSB[4]], writes=[bwS[par]])
        k.act(AF.Exp, wS[par], wS[par], reads=[bwS[par]], writes=[bwS[par]])
        k.mm(PSB[4], ps[4][0:64, 48:64], onesf_r[0:1, 0:64], cqrow, [bcq],
             last=True)
        k.act(AF.Exp, decay[par][0:64, :], ps[4][0:64, 48:64], reads=[PSB[4]],
              writes=[bdecay[par]])
        k.transp(PSB[4], ps[4][:, 96:128].bitcast(BF16)[:, 0:64],
                 xbc_v[0:64, 16, c0:c0 + Q], identb[0:64, 0:64],
                 [bxbc, bconst], pub=True)
        k.copy(bT[par], ps[4][:, 96:128].bitcast(BF16)[:, 0:64],
               reads=[PSB[4]], writes=[bbT[par]], eng="act")
        for h in range(NH):
            bk = 6 + h // 8
            dstp = ps[bk][:, (h % 8) * 64:(h % 8) * 64 + 64].bitcast(BF16)
            k.transp(PSB[bk], dstp, xbc_v[:, h, c0:c0 + Q], identb,
                     [bxbc, bconst], pub=(h % 8 == 7))
        k.copy(xT[0][:, 0:8 * Q], ps[6].bitcast(BF16)[:, 0:8 * Q],
               reads=[PSB[6]], writes=[bxT[0]], eng="act")
        k.copy(xT[0][:, 8 * Q:16 * Q], ps[7].bitcast(BF16)[:, 0:8 * Q],
               reads=[PSB[7]], writes=[bxT[0]], eng="act")
        release(6), release(7)
        if do_y:
            k.copy(Hbf[0:64, :], H[0:64, :], reads=[bH], writes=[bHbf])
            for q4 in range(4):
                for hh in range(4):
                    h = q4 * 4 + hh
                    k.mm(PSB[5], ps[5][0:1, hh * Q:(hh + 1) * Q], onesb_c,
                         dta[par][:, h * Q:(h + 1) * Q], [bdta[par], bconst],
                         last=(hh == 3))
                k.act(AF.Exp, erow[0:1, q4 * 512:(q4 + 1) * 512],
                      ps[5][0:1, 0:512], reads=[PSB[5]], writes=[berow])
                release(5)
            k.copy(ccpy[0:64, :], xbc_v[64:128, 16, c0:c0 + Q],
                   reads=[bxbc], writes=[bccpy])
            for rep in range(4):
                k.mm(PSB[5], ps[5][:, rep * 128:(rep + 1) * 128],
                     xbc_v[0:64, 16, c0:c0 + Q], ccpy[0:64, :],
                     [bxbc, bccpy], last=(rep == 3), pub=True)
        def stageA(q):
            qp = q % 2
            dbk, ebk = ((2, 3) if qp == 0 else (6, 7))
            if do_y:
                for hh in range(4):
                    h = q * 4 + hh
                    k.mm(PSB[dbk], ps[dbk][:, hh * Q:(hh + 1) * Q], onesb,
                         dta[par][:, h * Q:(h + 1) * Q], [bdta[par], bconst],
                         last=False, pub=False)
                    k.mm(PSB[dbk], ps[dbk][:, hh * Q:(hh + 1) * Q],
                         dta[par][:, h * Q:(h + 1) * Q], negonesb,
                         [bdta[par], bconst], last=False, stop=(hh == 3),
                         pub=(hh == 3))
                for hh in range(4):
                    h = q * 4 + hh
                    k.mm(PSB[ebk], ps[ebk][0:64, hh * Q:(hh + 1) * Q],
                         onesb_r[0:1, 0:64], erow[0:1, h * Q:(h + 1) * Q],
                         [berow, bconst], last=(hh == 3), pub=(hh == 3))
                k.stt(lmask4[qp], ps[dbk][:, 0:512], 0.0, maskneg4, ALU.min,
                      ALU.add, reads=[PSB[dbk], bconst], writes=[blm[qp]])
                k.act(AF.Exp, expl4[qp], lmask4[qp], reads=[],
                      writes=[blm[qp]])
                k.tt(stl4[qp], expl4[qp], ps[5][:, 0:512], ALU.mult,
                     reads=[blm[qp], PSB[5]], writes=[bstl[qp]])
                for hh in range(4):
                    k.tt(cw4[qp][0:64, hh * Q:(hh + 1) * Q],
                         xbc_v[64:128, 16, c0:c0 + Q],
                         ps[ebk][0:64, hh * Q:(hh + 1) * Q], ALU.mult,
                         reads=[bxbc, PSB[ebk]], writes=[bcw[qp]])
            for hh in range(4):
                h = q * 4 + hh
                k.ts1(bw4[qp][:, hh * DS:(hh + 1) * DS], bT[par],
                      wS[par][:, h:h + 1], ALU.mult,
                      reads=[bbT[par], bwS[par]], writes=[bbw[qp]])

        def stageB(q):
            qp = q % 2
            dbk, ebk = ((2, 3) if qp == 0 else (6, 7))
            qb = q % 2
            release(dbk)
            for hh in range(4):
                h = q * 4 + hh
                k.mm(PSB[dbk], ps[dbk][0:64, hh * Q:(hh + 1) * Q],
                     bw4[qp][:, hh * DS:(hh + 1) * DS],
                     xT[0][:, h * Q:(h + 1) * Q], [bbw[qp], bxT[0]],
                     last=(hh == 3), pub=(hh == 3))
            if do_y:
                for hh in range(4):
                    h = q * 4 + hh
                    k.mm(PSB[qb], ps[qb][:, hh * Q:(hh + 1) * Q],
                         xT[0][:, h * Q:(h + 1) * Q],
                         stl4[qp][:, hh * Q:(hh + 1) * Q],
                         [bxT[0], bstl[qp]], last=False, pub=(hh == 3))
                    k.mm(PSB[qb], ps[qb][:, hh * Q:(hh + 1) * Q],
                         Hbf[0:64, h * Q:(h + 1) * Q],
                         cw4[qp][0:64, hh * Q:(hh + 1) * Q],
                         [bHbf, bcw[qp]], last=False, stop=(hh == 3),
                         pub=(hh == 3))
            for hh in range(4):
                h = q * 4 + hh
                k.stt(H[0:64, h * Q:(h + 1) * Q], H[0:64, h * Q:(h + 1) * Q],
                      decay[par][0:64, h:h + 1],
                      ps[dbk][0:64, hh * Q:(hh + 1) * Q],
                      ALU.mult, ALU.add, reads=[bdecay[par], PSB[dbk]],
                      writes=[bH])
            release(dbk), release(ebk)
            if do_y:
                ysl = yg_v[:, q * 4:q * 4 + 4, c0:c0 + Q]
                k.tt(ysl, ysl, ps[qb][:, 0:512], ALU.add,
                     reads=[PSB[qb]], writes=[byg])
                release(qb)
                ych = yg_v[:, q * 4:q * 4 + 4, c0:c0 + Q]
                k.tt(sqs[:, 0:512], ych, ych, ALU.mult, reads=[byg],
                     writes=[bsqs], eng="dve")
                for hh in range(4):
                    k.mm(PSB[4], ps[4][0:1, 256:384], onesb_c,
                         sqs[:, hh * 128:(hh + 1) * 128], [bsqs],
                         last=False, stop=(q == 3 and hh == 3), pub=True)

        stageA(0)
        for q in range(4):
            if q + 1 < 4:
                stageA(q + 1)
            stageB(q)
        if _DEBUG and gc == 0 and not do_y:
            k.dma(dbg["dbg_ws"], wS[par], reads=[bwS[par]])
            k.dma(dbg["dbg_dec"], decay[par][0:64, :], reads=[bdecay[par]])
            k.dma(dbg["dbg_cta"], ctA[par], reads=[bctA[par]])
            k.dma(dbg["dbg_cq"], cqrow, reads=[bcq])
            k.dma(dbg["dbg_h0"], H[0:64, :], reads=[bH])
            k.dma(dbg["dbg_dtvp"], dtv[0:16, :], reads=[bdtv])
            k.dma(dbg["dbg_xbcp"], xbc, reads=[bxbc])
            k.dma(dbg["dbg_bt"], bT[par], reads=[bbT[par]])
            k.dma(dbg["dbg_xt"], xT[0], reads=[bxT[0]])
        if do_y:
            k.copy(sumsq[0:1, c0:c0 + Q], ps[4][0:1, 256:384], reads=[PSB[4]],
                   writes=[bsumsq], eng="act")
        release(4), release(5)

    def epilogue(g):
        w0 = g * GQ
        rstrow = rows[0:1, 0:512]
        k.act(AF.Ln, rstrow, sumsq, reads=[bsumsq, brows], writes=[brst],
              scale=1.0 / DI, bias=1e-5)
        k.act(AF.Exp, rstrow, rstrow, reads=[], writes=[brst], scale=-0.5)
        k.mm(PSB[2], ps[2][:, 0:512], onesf_r, rstrow, [brst], last=True)
        k.copy(rstd_bc, ps[2][:, 0:512], reads=[PSB[2]], writes=[brstdbc])
        release(2)
        for zc in range(NZC):
            zp = zc % 2
            k.dma(wzp[zp], D["wz"][zc].rearrange("p a b -> p (a b)"),
                  writes=[bwzp[zp]], eng="pool")
            for fc in range(NFC):
                k.mm(PSB[zp], ps[zp][:, 0:512], wzp[zp][:, fc * 128:(fc + 1) * 128],
                     xn_v[:, fc, 3 + w0:3 + w0 + 512], [bwzp[zp], bxn],
                     last=(fc == 7))
            k.act(AF.Sigmoid, zsig[zp], ps[zp][:, 0:512], reads=[PSB[zp]],
                  writes=[bzsig[zp]])
            release(zp)
            k.stt(yg_v[:, zc, :], yg_v[:, zc, :], rmsw[:, zc:zc + 1],
                  zsig[zp], ALU.mult, ALU.mult,
                  reads=[bconst, bzsig[zp]], writes=[byg])
        for dc in range(NDC):
            dp = dc % 2
            k.dma(wop[dp], D["wo"][dc].rearrange("p a b -> p (a b)"),
                  writes=[bwop[dp]], eng="pool")
            for kc in range(NKC):
                k.mm(PSB[dp], ps[dp][:, 0:512],
                     wop[dp][:, kc * 128:(kc + 1) * 128], yg_v[:, kc, :],
                     [bwop[dp], byg], last=(kc == 15))
            k.tt(ostage[dp], rstd_bc, ps[dp][:, 0:512], ALU.mult,
                 reads=[brstdbc, PSB[dp]], writes=[bost[dp]])
            release(dp)
            k.tt(ostage[dp], ostage[dp], ug_v[:, dc, 3:515], ALU.add,
                 reads=[bug], writes=[bost[dp]])
            dst = scratch.rearrange("(a p) n -> p a n", p=128)[:, dc, w0:w0 + 512]
            k.dma(dst, ostage[dp], reads=[bost[dp]], writes=[bscr])

    for g in range(NG):
        front_stats(D["u_prev"], g, xn_v, bxn, 0, do_gates=False)
        front_mm(xn_v, bxn, 0)
        for c in range(NCH):
            chunk(g * NCH + c, c, do_y=False)
    if _DEBUG:
        k.dma(dbg["dbg_hp"], H[0:64, :], reads=[bH])
    k.ts1(H[0:64, :], H[0:64, :], smask_s[0:64, 0:1], ALU.mult,
          reads=[bconst], writes=[bH])
    for g in range(NG):
        front_stats(D["u_own"], g, xn_v, bxn, g * GQ, do_gates=True)
        front_mm(xn_v, bxn, g * GQ, do_y=True)
        for c in range(NCH):
            chunk(NG * NCH + g * NCH + c, c, do_y=True)
        if _DEBUG and g == _DBG_G:
            k.dma(dbg["dbg_xbc"], xbc, reads=[bxbc])
            k.dma(dbg["dbg_dt"], dtv[0:16, :], reads=[bdtv])
            k.dma(dbg["dbg_y"], yg, reads=[byg])
        epilogue(g)
    if _DEBUG:
        k.dma(dbg["dbg_xn"], xn, reads=[bxn])
        k.dma(dbg["dbg_gates"], gates[0:8, :], reads=[bgates])
        k.dma(dbg["dbg_h"], H[0:64, :], reads=[bH])

    k.dma(acc_v, scratch.rearrange("(a p) n -> p a n", p=128), reads=[bscr],
          writes=[bacc])
    k.dma(b1c, D["b1"], writes=[bconst])
    k.dma(b2c, D["b2"], writes=[bconst], eng="pool")
    if _DEBUG:
        k.dma(dbg["dbg_scr"], scratch, reads=[bscr])
    k.memset(combtok, 0.5, writes=[bcombtok])
    k.memset(comb10[0:16, :], 0.5, writes=[bcomb10])
    for tc in range(16):
        k.transp(PSB[4], ps[4][:, 0:8], gates[0:8, tc * 128:(tc + 1) * 128],
                 identf[0:8, 0:8], [bgates], pub=True)
        k.copy(gtt, ps[4][:, 0:8], reads=[PSB[4]], writes=[bgtt])
        k.red(gsm[:, 0:1], gtt, ALU.max, reads=[bgtt], writes=[bgsm])
        k.ts1(is1t, gtt, gsm[:, 0:1], ALU.is_ge, reads=[bgtt, bgsm],
              writes=[bis1])
        k.stt(g2t, is1t, -1e30, gtt, ALU.mult, ALU.add,
              reads=[bis1, bgtt], writes=[bg2])
        k.red(gsm[:, 1:2], g2t, ALU.max, reads=[bg2], writes=[bgsm])
        k.ts1(is2t, g2t, gsm[:, 1:2], ALU.is_ge, reads=[bg2, bgsm],
              writes=[bis2])
        k.tt(gsm[:, 2:3], gsm[:, 0:1], gsm[:, 1:2], ALU.subtract,
             reads=[bgsm], writes=[bgsm])
        k.act(AF.Sigmoid, gsm[:, 3:4], gsm[:, 2:3], reads=[bgsm], writes=[bgsm])
        k.ts2(gsm[:, 4:5], gsm[:, 3:4], -1.0, 1.0, ALU.mult, ALU.add,
              reads=[bgsm], writes=[bgsm])
        k.ts1(tmp8, is1t, gsm[:, 3:4], ALU.mult, reads=[bis1, bgsm],
              writes=[btmp8])
        k.stt(combtok[:, tc * 10:tc * 10 + 8], is2t, gsm[:, 4:5], tmp8,
              ALU.mult, ALU.add, reads=[bis2, bgsm, btmp8], writes=[bcombtok])
        k.transp(PSB[4], ps[4][0:10, 128:256],
                 combtok[:, tc * 10:tc * 10 + 10], identf, [bcombtok],
                 pub=True)
        k.copy(comb10[0:10, tc * 128:(tc + 1) * 128], ps[4][0:10, 128:256],
               reads=[PSB[4]], writes=[bcomb10])
        release(4)
    for dc in range(NDC):
        for tq in range(4):
            k.mm(PSB[6], ps[6][:, 0:512], b2c[0:16, dc * 128:(dc + 1) * 128],
                 comb10[0:16, tq * 512:(tq + 1) * 512], [bconst, bcomb10],
                 last=True)
            k.tt(acc_v[:, dc, tq * 512:(tq + 1) * 512],
                 acc_v[:, dc, tq * 512:(tq + 1) * 512], ps[6][:, 0:512],
                 ALU.add, reads=[PSB[6]], writes=[bacc])
            release(6)
    for e in range(10):
        ep = e % 2
        if e < 8:
            for q4 in range(4):
                for i in range(4):
                    tc = q4 * 4 + i
                    k.transp(PSB[4], ps[4][0:1, i * 128:(i + 1) * 128],
                             combtok[:, tc * 10 + e:tc * 10 + e + 1], identf,
                             [bcombtok], pub=(i == 3))
                k.copy(combrow[0:1, q4 * 512:(q4 + 1) * 512],
                       ps[4][0:1, 0:512], reads=[PSB[4]], writes=[bcombrow])
                release(4)
            for tq in range(4):
                k.mm(PSB[5], ps[5][:, 0:512], onesb_r,
                     combrow[0:1, tq * 512:(tq + 1) * 512], [bcombrow],
                     last=True)
                k.act(AF.Copy, combbc[ep][:, tq * 512:(tq + 1) * 512],
                      ps[5][:, 0:512], reads=[PSB[5]], writes=[bcombbc[ep]])
                release(5)
        for hc in range(NHC):
            wp, bwp = w1p[hc % 3], bw1p[hc % 3]
            k.dma(wp, D["w1"][e, hc].rearrange("p a b -> p (a b)"),
                  writes=[bwp], eng="pool")
            for tq in range(4):
                pb = (hc * 4 + tq) % 2
                for fc in range(NFC):
                    k.mm(PSB[pb], ps[pb][:, 0:512],
                         wp[:, fc * 128:(fc + 1) * 128],
                         xn_v[:, fc, 3 + tq * 512:3 + tq * 512 + 512],
                         [bwp, bxn], last=(fc == 7))
                hp = tq % 2
                k.act(AF.Gelu, hraw[hp], ps[pb][:, 0:512], reads=[PSB[pb]],
                      writes=[bhraw[hp]],
                      bias=b1c[:, e * 16 + hc:e * 16 + hc + 1])
                release(pb)
                if e < 8:
                    k.tt(hsb_v[:, hc, tq * 512:(tq + 1) * 512], hraw[hp],
                         combbc[ep][:, tq * 512:(tq + 1) * 512], ALU.mult,
                         reads=[bhraw[hp], bcombbc[ep]], writes=[bhsb],
                         eng="dve")
                else:
                    k.ts1(hsb_v[:, hc, tq * 512:(tq + 1) * 512], hraw[hp],
                          0.5, ALU.mult, reads=[bhraw[hp]], writes=[bhsb],
                          eng="dve")
        for dc in range(NDC):
            wp2, bp2 = w2p[dc % 2], bw2p[dc % 2]
            k.dma(wp2, D["w2"][e, dc].rearrange("p a b -> p (a b)"),
                  writes=[bp2], eng="pool")
            for tq in range(4):
                pb = 2 + (dc * 4 + tq) % 2
                for hc in range(NHC):
                    k.mm(PSB[pb], ps[pb][:, 0:512],
                         wp2[:, hc * 128:(hc + 1) * 128],
                         hsb_v[:, hc, tq * 512:(tq + 1) * 512], [bp2, bhsb],
                         last=(hc == 15))
                k.tt(acc_v[:, dc, tq * 512:(tq + 1) * 512],
                     acc_v[:, dc, tq * 512:(tq + 1) * 512], ps[pb][:, 0:512],
                     ALU.add, reads=[PSB[pb]], writes=[bacc])
                release(pb)
    k.dma(out_d.rearrange("(a p) n -> p a n", p=128), acc_v, reads=[bacc],
          writes=[bout])
    k.S.wait("sp", bout.wtok)

    k.S.emit(nc, sems)
    return nc, list(dbg.keys())


_CACHE = {}


def kernel(**inputs):
    u = np.asarray(inputs["L0_attn_u"], np.float32)
    shared = _prep_shared(inputs)
    if "nc" not in _CACHE:
        _CACHE["nc"] = build_program()
    nc, dbg_names = _CACHE["nc"]
    in_maps = []
    for c in range(8):
        m = dict(shared)
        m.update(_prep_core(u, c))
        in_maps.append(m)
    res = run_bass_kernel_spmd(nc, in_maps, core_ids=list(range(8)))
    out = np.empty((4, 4096, 1024), np.float32)
    for c in range(8):
        b, half = divmod(c, 2)
        out[b, half * TPC:(half + 1) * TPC, :] = res.results[c]["out"].T
    if _DEBUG:
        kernel.dbg = res.results
    return out
